# version 10
# baseline (speedup 1.0000x reference)
import numpy as np
import ml_dtypes
import concourse.bass as bass
import concourse.mybir as mybir
from concourse.bass_utils import run_bass_kernel_spmd
from contextlib import ExitStack

F32 = mybir.dt.float32
BF16 = mybir.dt.bfloat16
U8 = mybir.dt.uint8
AF = mybir.ActivationFunctionType
ALU = mybir.AluOpType
AX = mybir.AxisListType
BFNP = ml_dtypes.bfloat16

NCORE = 8
B_LOC = 2
L = 4096
CDIM = 1024
DIN = 2048
TOK = B_LOC * L
ST = 512
NSUP = TOK // ST
CH = 512
NCH = L // CH
UCAP = 23
NTT = TOK // 128

DEBUG = False
T_SCAT = True
T_WG = True
T_BG = True
T_UNS = True

ENGINES = ("t", "v", "s", "q", "g")


class Buf:
    __slots__ = ("w", "readers", "name")
    def __init__(self, name):
        self.name = name
        self.w = None
        self.readers = set()


class Prog:
    def __init__(self, nc, sems):
        self.nc = nc
        self.sems = sems
        self.cmds = {e: [] for e in ENGINES}
        self.cnt = {e: 0 for e in ENGINES}
        self.lastidx = {e: -1 for e in ENGINES}
        self.markidx = {e: -1 for e in ENGINES}
        self.waited = {}

    def _emit(self, e, fn):
        self.cmds[e].append([fn, 0])
        self.lastidx[e] = len(self.cmds[e]) - 1
        return self.lastidx[e]

    def mark(self, e):
        li = self.lastidx[e]
        if li < 0:
            return (e, 0)
        if self.markidx[e] == li:
            return (e, self.cnt[e])
        inc = 16 if e in ("q", "g") else 1
        self.cmds[e][li][1] += inc
        self.cnt[e] += inc
        self.markidx[e] = li
        return (e, self.cnt[e])

    def wait(self, consumer, evt):
        pe, val = evt
        if val <= 0 or pe == consumer:
            return
        key = (consumer, pe)
        if self.waited.get(key, 0) >= val:
            return
        self.waited[key] = val
        sem = self.sems[pe]
        self.cmds[consumer].append(
            [lambda eng, s=sem, v=val: eng.wait_ge(s, v), None])

    def _hazards(self, e, reads, writes):
        for b in reads:
            if b.w is not None and b.w != e:
                self.wait(e, self.mark(b.w))
        for b in writes:
            if b.w is not None and b.w != e:
                self.wait(e, self.mark(b.w))
            for r in list(b.readers):
                if r != e:
                    self.wait(e, self.mark(r))
        for b in reads:
            b.readers.add(e)
        for b in writes:
            b.w = e
            b.readers = set()

    def op(self, e, fn, reads=(), writes=()):
        self._hazards(e, reads, writes)
        return self._emit(e, fn)

    def dma(self, e, out, in_, reads=(), writes=()):
        assert e in ("q", "g")
        self._hazards(e, reads, writes)
        idx = self._emit(e, lambda eng, o=out, i=in_: eng.dma_start(out=o, in_=i))
        self.cmds[e][idx][1] = 16
        self.cnt[e] += 16
        self.markidx[e] = idx

    def dma_fn(self, e, fn, reads=(), writes=()):
        self._hazards(e, reads, writes)
        idx = self._emit(e, fn)
        self.cmds[e][idx][1] = 16
        self.cnt[e] += 16
        self.markidx[e] = idx

    def barrier(self):
        evts = [self.mark(e) for e in ENGINES]
        for c in ENGINES:
            for evt in evts:
                self.wait(c, evt)


def build_nc():
    nc = bass.Bass("TRN2", target_bir_lowering=False)
    scratch_kind = "ExternalOutput" if DEBUG else "Internal"

    d_x = nc.dram_tensor("x", [TOK, CDIM], F32, kind="ExternalInput").ap()
    d_out = nc.dram_tensor("out", [TOK, CDIM], F32, kind="ExternalOutput").ap()

    d_winT = nc.dram_tensor("winT", [128, 8 * 4096], F32, kind="ExternalInput").ap()
    d_woutT = nc.dram_tensor("woutT", [128, 16 * 1024], F32, kind="ExternalInput").ap()
    d_xprojT = nc.dram_tensor("xprojT", [128, 2 * 1536], F32, kind="ExternalInput").ap()
    d_dtprojT = nc.dram_tensor("dtprojT", [64, 2 * 2048], F32, kind="ExternalInput").ap()
    d_convw = nc.dram_tensor("convw", [128, 2 * 64], F32, kind="ExternalInput").ap()
    d_convb = nc.dram_tensor("convb", [128, 2 * 16], F32, kind="ExternalInput").ap()
    d_dtb = nc.dram_tensor("dtb", [128, 2 * 16], F32, kind="ExternalInput").ap()
    d_Alog = nc.dram_tensor("Alog", [128, 2 * 256], F32, kind="ExternalInput").ap()
    d_Dv = nc.dram_tensor("Dv", [128, 2 * 16], F32, kind="ExternalInput").ap()
    d_wred = nc.dram_tensor("wred", [128, 128], F32, kind="ExternalInput").ap()
    d_wg = nc.dram_tensor("wg", [8, 16], F32, kind="ExternalInput").ap()
    d_W1 = nc.dram_tensor("W1", [8, 1024, 2048], F32, kind="ExternalInput").ap()
    d_b1c = nc.dram_tensor("b1c", [128, 128], F32, kind="ExternalInput").ap()
    d_W2 = nc.dram_tensor("W2", [8, 2048, 1024], F32, kind="ExternalInput").ap()
    d_b2c = nc.dram_tensor("b2c", [128, 64], F32, kind="ExternalInput").ap()
    d_iotap = nc.dram_tensor("iotap", [128, 2], F32, kind="ExternalInput").ap()
    d_evrow8 = nc.dram_tensor("evrow8", [128, 8], F32, kind="ExternalInput").ap()
    d_u24 = nc.dram_tensor("u24", [8, 24], F32, kind="ExternalInput").ap()
    d_eye = nc.dram_tensor("eye_f32", [128, 128], F32, kind="ExternalInput").ap()
    d_eyebf = nc.dram_tensor("eye_bf", [128, 128], BF16, kind="ExternalInput").ap()
    d_ones = nc.dram_tensor("ones_f32", [128, 128], F32, kind="ExternalInput").ap()
    d_onesbf = nc.dram_tensor("ones_bf", [16, 512], BF16, kind="ExternalInput").ap()
    d_sel = nc.dram_tensor("sel_bf", [16, 16 * 128], BF16, kind="ExternalInput").ap()

    if DEBUG:
        d_DBG = nc.dram_tensor("DBG", [16, 128, 512], F32, kind="ExternalOutput").ap()
        d_DBGU8 = nc.dram_tensor("DBGU8", [128, 8192], U8, kind="ExternalOutput").ap()
        d_DBGBF = nc.dram_tensor("DBGBF", [8, 128, 512], BF16, kind="ExternalOutput").ap()
    else:
        d_DBG = d_DBGU8 = d_DBGBF = None
    d_R1 = nc.dram_tensor("R1", [CDIM, TOK], F32, kind=scratch_kind).ap()
    d_R2 = nc.dram_tensor("R2", [CDIM, TOK], F32, kind=scratch_kind).ap()
    d_XS = nc.dram_tensor("XS", [B_LOC, DIN, L], F32, kind=scratch_kind).ap()
    d_ZS = nc.dram_tensor("ZS", [B_LOC, DIN, L], F32, kind=scratch_kind).ap()
    d_YF = nc.dram_tensor("YF", [B_LOC, DIN, L], F32, kind=scratch_kind).ap()
    d_W1B = nc.dram_tensor("W1B", [8, 1024, 2048], BF16, kind="Internal").ap()
    d_W2B = nc.dram_tensor("W2B", [8, 2048, 1024], BF16, kind="Internal").ap()
    d_XSRT = nc.dram_tensor("XSRT", [UCAP * ST, CDIM], F32, kind=scratch_kind).ap()
    d_OSRT = nc.dram_tensor("OSRT", [UCAP * ST, CDIM], F32, kind=scratch_kind).ap()
    d_X3 = nc.dram_tensor("X3", [TOK, CDIM], F32, kind=scratch_kind).ap()

    with ExitStack() as stk:
        def sbuf(name, shape, dt):
            return stk.enter_context(nc.sbuf_tensor(name, shape, dt))
        def psum(name, shape, dt):
            return stk.enter_context(nc.psum_tensor(name, shape, dt))

        AF8 = sbuf("AF8", [128, 14400], F32)
        AF8bf = AF8.bitcast(BF16)
        BF8 = sbuf("BF8", [128, 34816], BF16)
        BF2a = sbuf("BF2a", [128, 8192], BF16)
        BF2b = sbuf("BF2b", [128, 2048], BF16)
        U8a = sbuf("U8a", [128, 1024], U8)
        SM = sbuf("SM", [128, 2560], F32)
        BFS = sbuf("BFS", [128, 4096], BF16)
        I32 = mybir.dt.int32
        g_eid = sbuf("g_eid", [128, 64], F32)
        g_rank = sbuf("g_rank", [128, 64], F32)
        g_ssb = sbuf("g_ssb", [128, 64], F32)
        g_dest = sbuf("g_dest", [128, 64], F32)
        g_desti = sbuf("g_desti", [128, 64], I32)
        g_scale = sbuf("g_scale", [128, 64], F32)
        g_expu = sbuf("g_expu", [128, 24], F32)
        g_w1i = sbuf("g_w1i", [128, 24], I32)
        g_w2i = sbuf("g_w2i", [128, 24], I32)
        g_bii = sbuf("g_bii", [128, 24], I32)
        g_vb = sbuf("g_vb", [128, 8], F32)
        g_s8 = sbuf("g_s8", [8, 128], F32)
        c_one8 = sbuf("c_one8", [8, 512], F32)
        c_iotap = sbuf("c_iotap", [128, 2], F32)
        c_evrow8 = sbuf("c_evrow8", [128, 8], F32)
        c_u24 = sbuf("c_u24", [8, 24], F32)
        c_b1u = sbuf("c_b1u", [128, 16], F32)
        c_b2u = sbuf("c_b2u", [128, 8], F32)
        c_b1all = sbuf("c_b1all", [128, 128], F32)
        c_b2all = sbuf("c_b2all", [128, 64], F32)
        c_ind = sbuf("c_ind", [128, 8], F32)
        c_sp = sbuf("c_sp", [128, 32], F32)
        c_z = sbuf("c_z", [128, 1], F32)
        c_eye = sbuf("c_eye", [128, 128], F32)
        c_eyebf = sbuf("c_eyebf", [128, 128], BF16)
        c_ones = sbuf("c_ones", [128, 128], F32)
        c_onesbf = sbuf("c_onesbf", [16, 512], BF16)
        c_sel = sbuf("c_sel", [16, 16 * 128], BF16)
        c_wred = sbuf("c_wred", [128, 128], F32)
        c_convw = sbuf("c_convw", [128, 128], F32)
        c_convb = sbuf("c_convb", [128, 32], F32)
        c_dtb = sbuf("c_dtb", [128, 32], F32)
        c_Dv = sbuf("c_Dv", [128, 32], F32)
        c_A = sbuf("c_A", [128, 512], F32)
        c_xprojT = sbuf("c_xprojT", [128, 2 * 1536], BF16)
        c_dtprojT = sbuf("c_dtprojT", [64, 2 * 2048], BF16)
        c_wgn = sbuf("c_wgn", [8, 16], F32)
        c_wgnT = sbuf("c_wgnT", [16, 8], F32)
        c_carry = sbuf("c_carry", [128, 256], F32)
        c_k = sbuf("c_k", [128, 4], F32)

        pA = psum("pA", [128, 512], F32)
        pB = psum("pB", [128, 512], F32)
        pC = psum("pC", [128, 512], F32)
        pX = [psum(f"pX{i}", [128, 512], F32) for i in range(4)]
        pT = psum("pT", [128, 512], F32)

        sems = {e: stk.enter_context(nc.semaphore(f"sem_{e}")) for e in ENGINES}
        p = Prog(nc, sems)

        bufs = {}
        def B_(name):
            if name not in bufs:
                bufs[name] = Buf(name)
            return bufs[name]
        bconst = B_("const")
        bSM = B_("SM")
        bpA, bpB, bpC, bpT = B_("pA"), B_("pB"), B_("pC"), B_("pT")
        bpX = [B_(f"pX{i}") for i in range(4)]
        bcarry = B_("carry")

        def mm(ps_ap, lhsT, rhs, start, stop, wps, rds):
            p.op("t", lambda t, o=ps_ap, l=lhsT, r=rhs, a=start, b=stop:
                 t.matmul(o, l, r, start=a, stop=b), reads=rds, writes=[wps])

        def tr(ps_ap, in_ap, ident, wps, rds):
            p.op("t", lambda t, o=ps_ap, i=in_ap, e=ident:
                 t.transpose(o, i, e), reads=rds, writes=[wps])

        def act(out_ap, in_ap, func, wbuf, rds, bias=0.0, scale=1.0):
            p.op("s", lambda s, o=out_ap, i=in_ap, f=func, b=bias, sc=scale:
                 s.activation(o, i, f, bias=b, scale=sc), reads=rds, writes=[wbuf])

        def vcopy(out_ap, in_ap, wbuf, rds):
            p.op("v", lambda v, o=out_ap, i=in_ap: v.tensor_copy(o, i),
                 reads=rds, writes=[wbuf])

        def vtt(out_ap, a_ap, b_ap, alu, wbuf, rds):
            p.op("v", lambda v, o=out_ap, a=a_ap, b=b_ap, u=alu:
                 v.tensor_tensor(o, a, b, u), reads=rds, writes=[wbuf])

        def vts(out_ap, in_ap, s1, s2, op0, op1, wbuf, rds):
            if s2 is None:
                p.op("v", lambda v, o=out_ap, i=in_ap, a=s1, u0=op0:
                     v.tensor_scalar(o, i, a, None, u0), reads=rds, writes=[wbuf])
            else:
                p.op("v", lambda v, o=out_ap, i=in_ap, a=s1, b=s2, u0=op0, u1=op1:
                     v.tensor_scalar(o, i, a, b, u0, u1), reads=rds, writes=[wbuf])

        def vstt(out_ap, in0, scal, in1, op0, op1, wbuf, rds):
            p.op("v", lambda v, o=out_ap, i0=in0, sc=scal, i1=in1, u0=op0, u1=op1:
                 v.scalar_tensor_tensor(o, i0, sc, i1, u0, u1), reads=rds, writes=[wbuf])

        def gts(out_ap, in_ap, s1, op0, wbuf, rds):
            p.op("g", lambda g, o=out_ap, i=in_ap, a=s1, u0=op0:
                 g.tensor_scalar(o, i, a, None, u0), reads=rds, writes=[wbuf])

        def gtt(out_ap, a_ap, b_ap, alu, wbuf, rds):
            p.op("g", lambda g, o=out_ap, a=a_ap, b=b_ap, u=alu:
                 g.tensor_tensor(o, a, b, u), reads=rds, writes=[wbuf])

        def vscan(out_ap, d0, d1, init, wbuf, rds):
            p.op("v", lambda v, o=out_ap, a=d0, b=d1, i=init:
                 v.tensor_tensor_scan(o, a, b, i, ALU.mult, ALU.add),
                 reads=rds, writes=[wbuf])

        def vred(out_ap, in_ap, alu_max, wbuf, rds):
            if alu_max:
                p.op("v", lambda v, o=out_ap, i=in_ap:
                     v.reduce_max(o, i, axis=AX.X), reads=rds, writes=[wbuf])
            else:
                p.op("v", lambda v, o=out_ap, i=in_ap:
                     v.reduce_sum(o, i, axis=AX.X), reads=rds, writes=[wbuf])

        def memset(e, ap, val, wbuf):
            p.op(e, lambda eng, a=ap, c=val: eng.memset(a, c), writes=[wbuf])

        for dst, src in [
            (c_eye[:], d_eye), (c_eyebf[:], d_eyebf), (c_ones[:], d_ones),
            (c_onesbf[:], d_onesbf), (c_sel[:], d_sel),
            (c_wred[:], d_wred), (c_convw[:], d_convw),
            (c_convb[:], d_convb), (c_dtb[:], d_dtb), (c_Dv[:], d_Dv),
            (c_iotap[:], d_iotap), (c_evrow8[:], d_evrow8), (c_u24[:], d_u24),
            (c_b1all[:], d_b1c), (c_b2all[:], d_b2c),
        ]:
            p.dma("q", dst, src, writes=[bconst])
        memset("v", c_k[:, 0:1], 0.5, bconst)
        memset("v", c_k[:, 1:2], -0.5, bconst)
        memset("v", c_k[:, 2:3], -1.0, bconst)
        memset("v", c_k[:, 3:4], -0.6931471805599453, bconst)
        memset("v", c_one8[:], 1.0, bconst)
        memset("v", c_z[:], 0.0, bconst)
        p.dma("q", SM[:, 0:512], d_Alog, writes=[bSM])
        act(c_A[:], SM[:, 0:512], AF.Exp, bconst, [bSM])
        vts(c_A[:], c_A[:], -1.0, None, ALU.mult, ALU.bypass, bconst, [bconst])
        p.dma("q", SM[:, 0:1536], d_xprojT[:, 0:1536], writes=[bSM])
        vcopy(c_xprojT[:, 0:1536], SM[:, 0:1536], bconst, [bSM])
        p.dma("q", SM[:, 0:1536], d_xprojT[:, 1536:3072], writes=[bSM])
        vcopy(c_xprojT[:, 1536:3072], SM[:, 0:1536], bconst, [bSM])
        p.dma("q", SM[0:64, 0:2048], d_dtprojT[:, 0:2048], writes=[bSM])
        vcopy(c_dtprojT[:, 0:2048], SM[0:64, 0:2048], bconst, [bSM])
        p.dma("q", SM[0:64, 0:2048], d_dtprojT[:, 2048:4096], writes=[bSM])
        vcopy(c_dtprojT[:, 2048:4096], SM[0:64, 0:2048], bconst, [bSM])
        p.dma("q", c_wgn[:], d_wg, writes=[bconst])
        act(SM[0:8, 0:16], c_wgn[:], AF.Square, bSM, [bconst])
        vred(SM[0:8, 16:17], SM[0:8, 0:16], False, bSM, [bSM])
        act(SM[0:8, 17:18], SM[0:8, 16:17], AF.Sqrt, bSM, [bSM])
        vts(SM[0:8, 18:19], SM[0:8, 17:18], 1e-4, None, ALU.max, ALU.bypass, bSM, [bSM])
        act(SM[0:8, 21:22], SM[0:8, 18:19], AF.Ln, bSM, [bSM])
        act(SM[0:8, 22:23], SM[0:8, 0:1], AF.Square, bSM, [bSM])
        act(SM[0:8, 23:24], SM[0:8, 0:1], AF.Square, bSM, [bSM])
        act(SM[0:8, 19:20], SM[0:8, 21:22], AF.Exp, bSM, [bSM], scale=c_k[0:8, 2:3])
        vts(c_wgn[:], c_wgn[:], SM[0:8, 19:20], None, ALU.mult, ALU.bypass,
            bconst, [bconst, bSM])
        mm(pT[0:16, 0:8], c_wgn[:], c_eye[0:8, 0:8], True, True, bpT, [bconst])
        act(c_wgnT[:], pT[0:16, 0:8], AF.Copy, bconst, [bpT])

        bwc = [B_("wc0"), B_("wc1")]
        bwcb = [B_("wcb0"), B_("wcb1")]
        for src, dst, rows, w in ((d_W1, d_W1B, 1024, 2048), (d_W2, d_W2B, 2048, 1024)):
            for e in range(8):
                for tile in range(rows // 128):
                    i = tile % 2
                    sl = AF8[:, 4096 + i * w: 4096 + (i + 1) * w]
                    bfl = BFS[:, i * w:(i + 1) * w]
                    p.dma("q", sl, src[e, tile * 128:(tile + 1) * 128, :], writes=[bwc[i]])
                    vcopy(bfl, sl, bwcb[i], [bwc[i]])
                    p.dma("g", dst[e, tile * 128:(tile + 1) * 128, :], bfl, reads=[bwcb[i]])

        bxT = B_("xT")
        boacc = B_("oacc")
        bgh = B_("gh")
        blnbf = B_("lnbf")
        bmask = B_("mask")
        bw1 = B_("w1")
        bw2 = B_("w2")

        def ln_norm(get_x, routing, lnbf, bk1=None, bk2=None):
            pA_, bpA_ = bk1 if bk1 is not None else (pA, bpA)
            pB_, bpB_ = bk2 if bk2 is not None else (pB, bpB)
            for ct in range(8):
                xct, bx = get_x(ct)
                act(SM[:, 0:512], xct, AF.Square, bSM, [bx])
                mm(pA_[0:1, 0:512], c_ones[:, 0:1], xct, ct == 0, ct == 7, bpA_,
                   [bx, bconst])
                mm(pB_[0:1, 0:512], c_ones[:, 0:1], SM[:, 0:512], ct == 0, ct == 7,
                   bpB_, [bSM, bconst])
            A_ = SM[0:1, 1024:1536]
            Bs = SM[0:1, 1536:2048]
            Cs = SM[0:1, 2048:2560]
            act(A_, pA_[0:1, 0:512], AF.Copy, bSM, [bpA_])
            act(Bs, pB_[0:1, 0:512], AF.Copy, bSM, [bpB_])
            vts(A_, A_, 1.0 / CDIM, None, ALU.mult, ALU.bypass, bSM, [bSM])
            vts(Bs, Bs, 1.0 / CDIM, None, ALU.mult, ALU.bypass, bSM, [bSM])
            vtt(Cs, A_, A_, ALU.mult, bSM, [bSM])
            vtt(Bs, Bs, Cs, ALU.subtract, bSM, [bSM])
            vts(Bs, Bs, 1e-5, None, ALU.add, ALU.bypass, bSM, [bSM])
            act(Cs, Bs, AF.Ln, bSM, [bSM])
            act(Bs, Cs, AF.Exp, bSM, [bSM], scale=c_k[0:1, 1:2])
            vtt(Cs, A_, Bs, ALU.mult, bSM, [bSM])
            mm(pA_[:, 0:512], c_ones[0:1, :], Bs, True, True, bpA_, [bSM, bconst])
            mm(pB_[:, 0:512], c_ones[0:1, :], Cs, True, True, bpB_, [bSM, bconst])
            for ct in range(8):
                xct, bx = get_x(ct)
                t1 = SM[:, 0:512]
                vtt(t1, xct, pA_[:, 0:512], ALU.mult, bSM, [bx, bpA_])
                lnf = SM[:, 512:1024]
                vtt(lnf, t1, pB_[:, 0:512], ALU.subtract, bSM, [bSM, bpB_])
                vcopy(lnbf[:, ct * 512:(ct + 1) * 512], lnf, blnbf, [bSM])
                if routing:
                    mm(pC[0:16, 0:512], c_wred[:, ct * 16:(ct + 1) * 16], lnf,
                       ct == 0, ct == 7, bpC, [bSM, bconst])

        bG = B_("G")

        def routing_A(sup):
            xred = SM[0:16, 0:512]
            act(xred, pC[0:16, 0:512], AF.Copy, bSM, [bpC])
            mm(pC[0:8, 0:512], c_wgnT[:], xred, True, True, bpC, [bSM, bconst])
            lg = SM[0:8, 512:1024]
            act(lg, pC[0:8, 0:512], AF.Copy, bSM, [bpC])
            ltok = SM[:, 0:32]
            for tb in range(4):
                tr(pT[0:128, 0:8], lg[:, tb * 128:(tb + 1) * 128], c_eye[0:8, 0:8],
                   bpT, [bSM])
                act(ltok[:, tb * 8:(tb + 1) * 8], pT[0:128, 0:8], AF.Copy, bSM, [bpT])
            lmax = SM[:, 32:36]
            nmax = SM[:, 36:40]
            ee = SM[:, 40:72]
            ssum = SM[:, 72:76]
            maskf = SM[:, 80:112]
            cumf = SM[:, 112:144]
            eidt = SM[:, 144:176]
            for tb in range(4):
                lt = ltok[:, tb * 8:(tb + 1) * 8]
                vred(lmax[:, tb:tb + 1], lt, True, bSM, [bSM])
                p.op("v", lambda v, o=nmax[:, tb:tb + 1], i=lt:
                     v.reduce_max(o, i, axis=AX.X, negate=True),
                     reads=[bSM], writes=[bSM])
            for tb in range(4):
                lt = ltok[:, tb * 8:(tb + 1) * 8]
                act(ee[:, tb * 8:(tb + 1) * 8], lt, AF.Exp, bSM, [bSM],
                    bias=nmax[:, tb:tb + 1])
                vts(maskf[:, tb * 8:(tb + 1) * 8], lt, lmax[:, tb:tb + 1], None,
                    ALU.is_equal, ALU.bypass, bSM, [bSM])
            for tb in range(4):
                vred(ssum[:, tb:tb + 1], ee[:, tb * 8:(tb + 1) * 8], False, bSM, [bSM])
            for tb in range(4):
                vscan(cumf[:, tb * 8:(tb + 1) * 8], c_ones[:, 0:8],
                      maskf[:, tb * 8:(tb + 1) * 8], c_z[:, 0:1], bSM, [bSM, bconst])
            for tb in range(4):
                vts(eidt[:, tb * 8:(tb + 1) * 8], cumf[:, tb * 8:(tb + 1) * 8], 1.0,
                    None, ALU.is_equal, ALU.bypass, bSM, [bSM])
            for tb in range(4):
                mf = maskf[:, tb * 8:(tb + 1) * 8]
                vtt(mf, mf, eidt[:, tb * 8:(tb + 1) * 8], ALU.mult, bSM, [bSM])
            maskT = SM[0:8, 1024:1536]
            for tb in range(4):
                tr(pT[0:8, 0:128], maskf[:, tb * 8:(tb + 1) * 8], c_eye[:], bpT, [bSM])
                act(maskT[:, tb * 128:(tb + 1) * 128], pT[0:8, 0:128], AF.Copy, bSM, [bpT])
            cumT = SM[0:8, 2048:2560]
            vscan(cumT, c_one8[0:8, 0:512], maskT, g_s8[:, 0:1], bSM, [bSM, bconst, bG])
            for tb in range(4):
                tr(pT[0:128, 0:8], cumT[:, tb * 128:(tb + 1) * 128], c_eye[0:8, 0:8],
                   bpT, [bSM])
                act(cumf[:, tb * 8:(tb + 1) * 8], pT[0:128, 0:8], AF.Copy, bSM, [bpT])
            vcopy(g_s8[:, 0:1], cumT[:, 511:512], bG, [bSM])
            rkt = SM[:, 176:208]
            for tb in range(4):
                mf = maskf[:, tb * 8:(tb + 1) * 8]
                vtt(eidt[:, tb * 8:(tb + 1) * 8], mf, c_evrow8[:], ALU.mult, bSM,
                    [bSM, bconst])
                vtt(rkt[:, tb * 8:(tb + 1) * 8], mf, cumf[:, tb * 8:(tb + 1) * 8],
                    ALU.mult, bSM, [bSM])
            for tb in range(4):
                j = sup * 4 + tb
                vred(g_eid[:, j:j + 1], eidt[:, tb * 8:(tb + 1) * 8], False, bG, [bSM])
                vred(g_rank[:, j:j + 1], rkt[:, tb * 8:(tb + 1) * 8], False, bG, [bSM])
                vcopy(g_ssb[:, j:j + 1], ssum[:, tb:tb + 1], bG, [bSM])

        def vspace(n=4):
            for i in range(n):
                vcopy(c_sp[:, (i % 4) * 8:(i % 4) * 8 + 8], c_ones[:, 0:8],
                      B_("sp"), [bconst])

        def unit_plan():
            s8 = g_s8
            vts(s8[:, 1:2], s8[:, 0:1], 511.0, 1.0 / 512.0, ALU.add, ALU.mult, bG, [bG])
            vspace()
            memset("v", s8[:, 96:112], 0.0, bG)
            for k in range(1, 17):
                vts(s8[:, 95 + k:96 + k], s8[:, 1:2], float(k), None, ALU.is_ge,
                    ALU.bypass, bG, [bG])
            vspace()
            vred(s8[:, 2:3], s8[:, 96:112], False, bG, [bG])
            tr(pT[0:1, 0:8], s8[:, 2:3], c_eye[0:8, 0:8], bpT, [bG])
            act(s8[0:1, 8:16], pT[0:1, 0:8], AF.Copy, bG, [bpT])
            vscan(s8[0:1, 16:24], c_one8[0:1, 0:8], s8[0:1, 8:16], c_z[0:1, 0:1],
                  bG, [bG, bconst])
            vspace()
            vtt(s8[0:1, 24:32], s8[0:1, 16:24], s8[0:1, 8:16], ALU.subtract, bG, [bG])
            tr(pT[0:8, 0:1], s8[0:1, 24:32], c_eye[0:1, 0:1], bpT, [bG])
            act(s8[:, 4:5], pT[0:8, 0:1], AF.Copy, bG, [bpT])
            vtt(s8[:, 5:6], s8[:, 4:5], s8[:, 2:3], ALU.add, bG, [bG])
            vts(s8[:, 32:56], c_u24[:], s8[:, 4:5], None, ALU.is_ge, ALU.bypass,
                bG, [bconst, bG])
            vspace()
            vts(s8[:, 56:80], c_u24[:], s8[:, 5:6], None, ALU.is_lt, ALU.bypass,
                bG, [bconst, bG])
            vspace()
            vtt(s8[:, 32:56], s8[:, 32:56], s8[:, 56:80], ALU.mult, bG, [bG])
            vspace()
            vts(s8[:, 56:80], s8[:, 32:56], c_iotap[0:8, 0:1], None, ALU.mult,
                ALU.bypass, bG, [bG, bconst])
            vspace()
            mm(pT[0:1, 32:56], c_one8[:, 0:1], s8[:, 56:80], True, True, bpT,
               [bG, bconst])
            act(s8[0:1, 104:128], pT[0:1, 32:56], AF.Copy, bG, [bpT])
            mm(pT[:, 0:24], c_ones[0:1, :], s8[0:1, 104:128], True, True, bpT,
               [bG, bconst])
            act(g_expu[:], pT[:, 0:24], AF.Copy, bG, [bpT])
            mm(pT[:, 32:40], c_ones[0:1, :], s8[0:1, 24:32], True, True, bpT,
               [bG, bconst])
            act(g_vb[:], pT[:, 32:40], AF.Copy, bG, [bpT])
            memset("v", g_dest[:], 0.0, bG)
            for e in range(8):
                vts(SM[:, e * 64:(e + 1) * 64], g_eid[:], float(e), None,
                    ALU.is_equal, ALU.bypass, bSM, [bG])
            for e in range(8):
                vstt(g_dest[:], SM[:, e * 64:(e + 1) * 64], g_vb[:, e:e + 1],
                     g_dest[:], ALU.mult, ALU.add, bG, [bSM, bG])
            vspace()
            vts(g_dest[:], g_dest[:], 512.0, -1.0, ALU.mult, ALU.add, bG, [bG])
            vspace()
            vtt(g_dest[:], g_dest[:], g_rank[:], ALU.add, bG, [bG])
            vspace()
            vts(g_dest[:], g_dest[:], 0.0, float(UCAP * ST - 1), ALU.max, ALU.min,
                bG, [bG])
            vspace()
            vcopy(g_desti[:], g_dest[:], bG, [bG])
            act(SM[:, 512:576], g_ssb[:], AF.Ln, bSM, [bG])
            act(c_sp[:, 0:8], c_ones[:, 0:8], AF.Copy, B_("sp"), [bconst])
            act(c_sp[:, 8:16], c_ones[:, 0:8], AF.Copy, B_("sp"), [bconst])
            act(g_scale[:], SM[:, 512:576], AF.Exp, bG, [bSM], scale=c_k[:, 2:3],
                bias=c_k[:, 3:4])
            specs = ((1024.0, 0, g_w1i, 8191.0), (2048.0, 0, g_w2i, 16383.0),
                     (1.0, 1, g_bii, 1023.0))
            for i, (mulc, iot, dst, hi) in enumerate(specs):
                vts(SM[:, i * 32:i * 32 + 24], g_expu[:], mulc, None, ALU.mult,
                    ALU.bypass, bSM, [bG])
            vspace()
            for i, (mulc, iot, dst, hi) in enumerate(specs):
                vts(SM[:, i * 32:i * 32 + 24], SM[:, i * 32:i * 32 + 24],
                    c_iotap[:, iot:iot + 1], None, ALU.add, ALU.bypass, bSM,
                    [bSM, bconst])
            vspace()
            for i, (mulc, iot, dst, hi) in enumerate(specs):
                vts(SM[:, i * 32:i * 32 + 24], SM[:, i * 32:i * 32 + 24], 0.0, hi,
                    ALU.max, ALU.min, bSM, [bSM])
            vspace()
            for i, (mulc, iot, dst, hi) in enumerate(specs):
                vcopy(dst[:], SM[:, i * 32:i * 32 + 24], bG, [bSM])

        xT = AF8[:, 0:4096]
        o_f32 = AF8[:, 4096:8192]
        ght = AF8bf[:, 16384:24576]
        xrs = [AF8[:, 12288:13312], AF8[:, 13312:14336]]
        bxr = [B_("xr0"), B_("xr1")]
        xos = [AF8[:, 0:1024], AF8[:, 1024:2048]]
        bxo = [B_("xo0"), B_("xo1")]
        w1bufs = [BF8[:, 0:16384], BF8[:, 16384:32768]]
        w2bufs = [BF8[:, 32768:33792], BF8[:, 33792:34816]]
        bw1p = [B_("w1p0"), B_("w1p1")]
        bw2p = [B_("w2p0"), B_("w2p1")]
        IOOA = bass.IndirectOffsetOnAxis

        def gather_w1(u):
            par = u % 2
            for ct in range(8):
                if T_WG:
                    p.dma_fn("g", lambda g, o=w1bufs[par][:, ct * 2048:(ct + 1) * 2048],
                             ix=g_w1i[:, u:u + 1], eo=ct * 128 * 2048:
                             g.indirect_dma_start(out=o, out_offset=None, in_=d_W1B,
                                                  in_offset=IOOA(ix, 1), element_offset=eo,
                                                  bounds_check=None, oob_is_err=False),
                             reads=[bG], writes=[bw1p[par]])
                else:
                    p.dma("g", w1bufs[par][:, ct * 2048:(ct + 1) * 2048],
                          d_W1B[0, ct * 128:(ct + 1) * 128, :], reads=[bG],
                          writes=[bw1p[par]])

        def gather_w2slice(u, ht):
            sp = ht % 2
            if T_WG:
                p.dma_fn("g", lambda g, o=w2bufs[sp], ix=g_w2i[:, u:u + 1],
                         eo=ht * 128 * 1024:
                         g.indirect_dma_start(out=o, out_offset=None, in_=d_W2B,
                                              in_offset=IOOA(ix, 1), element_offset=eo,
                                              bounds_check=None, oob_is_err=False),
                         reads=[bG], writes=[bw2p[sp]])
            else:
                p.dma("g", w2bufs[sp], d_W2B[0, ht * 128:(ht + 1) * 128, :],
                      reads=[bG], writes=[bw2p[sp]])

        def gather_b(u):
            for e in range(8):
                vts(c_ind[:, e:e + 1], g_expu[:, u:u + 1], float(e), None,
                    ALU.is_equal, ALU.bypass, B_("ind"), [bG])
            for e in range(8):
                if e == 0:
                    vts(c_b1u[:], c_b1all[:, 0:16], c_ind[:, 0:1], None,
                        ALU.mult, ALU.bypass, B_("b1u"), [bconst, B_("ind")])
                    vts(c_b2u[:], c_b2all[:, 0:8], c_ind[:, 0:1], None,
                        ALU.mult, ALU.bypass, B_("b2u"), [bconst, B_("ind")])
                else:
                    vstt(c_b1u[:], c_b1all[:, e * 16:(e + 1) * 16], c_ind[:, e:e + 1],
                         c_b1u[:], ALU.mult, ALU.add, B_("b1u"),
                         [bconst, B_("ind"), B_("b1u")])
                    vstt(c_b2u[:], c_b2all[:, e * 8:(e + 1) * 8], c_ind[:, e:e + 1],
                         c_b2u[:], ALU.mult, ALU.add, B_("b2u"),
                         [bconst, B_("ind"), B_("b2u")])

        def p1_src(sup):
            for tk in range(4):
                xtile = SM[:, 0:1024]
                p.dma("q", xtile, d_x[sup * ST + tk * 128: sup * ST + (tk + 1) * 128, :],
                      writes=[bSM])
                for cb in range(8):
                    tr(pT[:, 0:128], xtile[:, cb * 128:(cb + 1) * 128], c_eye[:],
                       bpT, [bSM])
                    act(xT[:, cb * 512 + tk * 128: cb * 512 + (tk + 1) * 128],
                        pT[:, 0:128], AF.Copy, bxT, [bpT])
            def get(ct):
                return xT[:, ct * 512:(ct + 1) * 512], bxT
            return get

        def p4A_src(sup):
            for ct in range(8):
                p.dma("q", xT[:, ct * 512:(ct + 1) * 512],
                      d_R2[ct * 128:(ct + 1) * 128, sup * ST:(sup + 1) * ST],
                      writes=[bxT])
            for tk in range(4):
                xr = xrs[tk % 2]
                for cb in range(8):
                    tr(pT[:, 0:128], xT[:, cb * 512 + tk * 128: cb * 512 + (tk + 1) * 128],
                       c_eye[:], bpT, [bxT])
                    act(xr[:, cb * 128:(cb + 1) * 128], pT[:, 0:128], AF.Copy,
                        bxr[tk % 2], [bpT])
                p.dma("g", d_X3[(sup * 4 + tk) * 128:(sup * 4 + tk + 1) * 128, :], xr,
                      reads=[bxr[tk % 2]], writes=[B_("dX3")])
            def get(ct):
                return xT[:, ct * 512:(ct + 1) * 512], bxT
            return get

        def pass_C(src_tm):
            for j in range(NTT):
                xr = xrs[j % 2]
                p.dma("q", xr, src_tm[j * 128:(j + 1) * 128, :], writes=[bxr[j % 2]],
                      reads=[B_("dX3")] if src_tm is d_X3 else [])
                if T_SCAT:
                    p.dma_fn("g", lambda g, i=xr, ix=g_desti[:, j:j + 1]:
                             g.indirect_dma_start(out=d_XSRT, out_offset=IOOA(ix, 0),
                                                  in_=i, in_offset=None,
                                                  element_offset=0, bounds_check=None,
                                                  oob_is_err=False),
                             reads=[bxr[j % 2], bG], writes=[B_("dXS")])
                else:
                    p.dma("g", d_XSRT[j * 128:(j + 1) * 128, :], xr,
                          reads=[bxr[j % 2], bG], writes=[B_("dXS")])

        PS = [(pA, bpA), (pB, bpB), (pC, bpC), (pX[0], bpX[0]), (pX[1], bpX[1]),
              (pX[2], bpX[2]), (pX[3], bpX[3]), (pT, bpT)]

        def unit_D(u):
            par = u % 2
            if u == 0:
                gather_w1(0)
            gather_b(u)
            for r in range(4):
                xr = xrs[r % 2]
                p.dma("q", xr, d_XSRT[(u * 4 + r) * 128:(u * 4 + r + 1) * 128, :],
                      writes=[bxr[r % 2]], reads=[B_("dXS")])
                for cb in range(8):
                    tr(pT[:, 0:128], xr[:, cb * 128:(cb + 1) * 128], c_eye[:],
                       bpT, [bxr[r % 2]])
                    act(xT[:, cb * 512 + r * 128: cb * 512 + (r + 1) * 128],
                        pT[:, 0:128], AF.Copy, bxT, [bpT])
            def get(ct):
                return xT[:, ct * 512:(ct + 1) * 512], bxT
            ln_norm(get, False, BF2a, bk1=(pC, bpC), bk2=(pT, bpT))
            for ht in range(16):
                hp, bhp = (pA, bpA) if ht % 2 == 0 else (pB, bpB)
                for ct in range(8):
                    mm(hp[:, 0:512],
                       w1bufs[par][:, ct * 2048 + ht * 128: ct * 2048 + (ht + 1) * 128],
                       BF2a[:, ct * 512:(ct + 1) * 512],
                       ct == 0, ct == 7, bhp, [bw1p[par], blnbf])
                act(ght[:, ht * 512:(ht + 1) * 512], hp[:, 0:512], AF.Gelu,
                    bgh, [bhp, B_("b1u")], bias=c_b1u[:, ht:ht + 1])
            if u + 1 < UCAP:
                gather_w1(u + 1)
            for h2 in range(2):
                gather_w2slice(u, 0)
                gather_w2slice(u, 1)
                for ht in range(16):
                    sp = ht % 2
                    for c2 in range(4):
                        ct2 = h2 * 4 + c2
                        mm(pX[c2][:, 0:512], w2bufs[sp][:, ct2 * 128:(ct2 + 1) * 128],
                           ght[:, ht * 512:(ht + 1) * 512],
                           ht == 0, ht == 15, bpX[c2], [bw2p[sp], bgh])
                    if ht + 2 < 16:
                        gather_w2slice(u, ht + 2)
                for c2 in range(4):
                    ct2 = h2 * 4 + c2
                    vts(o_f32[:, ct2 * 512:(ct2 + 1) * 512], pX[c2][:, 0:512],
                        c_b2u[:, ct2:ct2 + 1], None, ALU.add, ALU.bypass,
                        boacc, [bpX[c2], B_("b2u")])
            for r in range(4):
                xr = xrs[r % 2]
                for cb in range(8):
                    tr(pT[:, 0:128], o_f32[:, cb * 512 + r * 128: cb * 512 + (r + 1) * 128],
                       c_eye[:], bpT, [boacc])
                    act(xr[:, cb * 128:(cb + 1) * 128], pT[:, 0:128], AF.Copy,
                        bxr[r % 2], [bpT])
                p.dma("g", d_OSRT[(u * 4 + r) * 128:(u * 4 + r + 1) * 128, :], xr,
                      reads=[bxr[r % 2]], writes=[B_("dOS")])

        def pass_E(phase):
            for sup in range(NSUP):
                for tk in range(4):
                    j = sup * 4 + tk
                    xr = xrs[j % 2]
                    xo = xos[j % 2]
                    if T_UNS:
                        p.dma_fn("g", lambda g, o=xr, ix=g_desti[:, j:j + 1]:
                                 g.indirect_dma_start(out=o, out_offset=None,
                                                      in_=d_OSRT,
                                                      in_offset=IOOA(ix, 0),
                                                      element_offset=0,
                                                      bounds_check=None,
                                                      oob_is_err=False),
                                 reads=[bG, B_("dOS")], writes=[bxr[j % 2]])
                    else:
                        p.dma("g", xr, d_OSRT[j * 128:(j + 1) * 128, :],
                              reads=[bG, B_("dOS")], writes=[bxr[j % 2]])
                    p.dma("q", xo,
                          (d_x if phase == 1 else d_X3)[j * 128:(j + 1) * 128, :],
                          writes=[bxo[j % 2]],
                          reads=[B_("dX3")] if phase == 4 else [])
                    vstt(xr, xr, g_scale[:, j:j + 1], xo, ALU.mult, ALU.add,
                         bxr[j % 2], [bxr[j % 2], bxo[j % 2], bG])
                    if phase == 1:
                        for cb in range(8):
                            tr(pT[:, 0:128], xr[:, cb * 128:(cb + 1) * 128], c_eye[:],
                               bpT, [bxr[j % 2]])
                            act(o_f32[:, cb * 512 + tk * 128: cb * 512 + (tk + 1) * 128],
                                pT[:, 0:128], AF.Copy, boacc, [bpT])
                    else:
                        p.dma("g", d_out[j * 128:(j + 1) * 128, :], xr,
                              reads=[bxr[j % 2]])
                if phase == 1:
                    for cb in range(8):
                        p.dma("g", d_R1[cb * 128:(cb + 1) * 128, sup * ST:(sup + 1) * ST],
                              o_f32[:, cb * 512:(cb + 1) * 512], reads=[boacc],
                              writes=[B_("dR1")])

        def ffn_sparse(phase):
            src_fn = p1_src if phase == 1 else p4A_src
            memset("v", g_s8[:, 0:1], 0.0, bG)
            for sup in range(NSUP):
                get_x = src_fn(sup)
                ln_norm(get_x, True, BF2a)
                routing_A(sup)
            p.barrier()
            unit_plan()
            if DEBUG and phase == 1:
                for i, t in enumerate((g_eid, g_rank, g_ssb, g_dest, g_scale)):
                    p.dma("g", d_DBG[i][:, 0:64], t[:], reads=[bG])
                p.dma("g", d_DBG[5][:, 0:24], g_expu[:], reads=[bG])
                p.dma("g", d_DBG[6][0:8, 0:128], g_s8[:], reads=[bG])
                p.dma("g", d_DBG[7][:, 0:24], g_w1i[:].bitcast(F32), reads=[bG])
            p.barrier()
            pass_C(d_x if phase == 1 else d_X3)
            p.barrier()
            for u in range(UCAP):
                unit_D(u)
            p.barrier()
            pass_E(phase)
            p.barrier()

        ffn_sparse(1)

        bscr = B_("scr")
        br1 = B_("r1")
        for ct in range(8):
            scr = AF8[:, 0:4096]
            p.dma("q", scr, d_winT[:, ct * 4096:(ct + 1) * 4096], writes=[bscr])
            vcopy(BF8[:, ct * 4096:(ct + 1) * 4096], scr, bw1, [bscr])
        for sup in range(NSUP):
            b = (sup * ST) // L
            t0 = (sup * ST) % L
            r1 = AF8[:, 4096:8192]
            for ct in range(8):
                p.dma("q", r1[:, ct * 512:(ct + 1) * 512],
                      d_R1[ct * 128:(ct + 1) * 128, sup * ST:(sup + 1) * ST],
                      writes=[br1])
            def get2(ct, r1=r1):
                return r1[:, ct * 512:(ct + 1) * 512], br1
            ln_norm(get2, False, BF2a)
            for mt in range(32):
                tgt = d_XS if mt < 16 else d_ZS
                dt_ = mt % 16
                for ct in range(8):
                    mm(pA[:, 0:512],
                       BF8[:, ct * 4096 + mt * 128: ct * 4096 + (mt + 1) * 128],
                       BF2a[:, ct * 512:(ct + 1) * 512], ct == 0, ct == 7,
                       bpA, [bw1, blnbf])
                ev = SM[:, 0:512] if mt % 2 == 0 else SM[:, 512:1024]
                act(ev, pA[:, 0:512], AF.Copy, bSM, [bpA])
                p.dma("g", tgt[b, dt_ * 128:(dt_ + 1) * 128, t0:t0 + 512], ev,
                      reads=[bSM])
        p.barrier()

        for half in range(2):
            scr = AF8[:, 0:8192]
            p.dma("q", scr, d_woutT[:, half * 8192:(half + 1) * 8192], writes=[bscr])
            vcopy(BF8[:, half * 8192:(half + 1) * 8192], scr, bw1, [bscr])
        woutT = BF8[:, 0:16384]
        xcbf = BF8[:, 16384:24576]
        yszb = BF8[:, 24576:32768]
        bxcbf = B_("xcbf")
        bysz = B_("ysz")

        XU = AF8[:, 0:8256]
        bXU = B_("XU")
        SLOT = [AF8[:, 8256 + i * 512: 8256 + (i + 1) * 512] for i in range(12)]
        bSL = [B_(f"sl{i}") for i in range(12)]
        Cb = BF2a
        bCb = B_("CbP3")
        sbs = [BF2b.bitcast(F32)[:, 0:512], BF2b.bitcast(F32)[:, 512:1024]]
        bsb = [B_("sb0"), B_("sb1")]
        h_ng = SM[:, 0:2048]
        bh = B_("h_ng")
        zero512 = SM[:, 2048:2560]
        bzero = B_("zero")
        dtc_bf = BFS[0:64, 0:512]
        bdtc = B_("dtc")
        Cv_bf = BFS[0:16, 512:1024]
        bCv = B_("Cv")
        Bv_bf = BFS[0:16, 1024:1536]
        bBv = B_("Bv")
        hC_big = BFS[:, 1536:3584]
        bhCb = B_("hCb")
        uDb = BFS[:, 3584:4096]
        buD = B_("uD")
        fin8 = U8a[:, 0:512]
        m28 = U8a[:, 512:1024]
        bu8f = B_("u8fin")
        bu8m = B_("u8m2")

        memset("v", zero512, 0.0, bzero)

        for b in range(B_LOC):
            for rev in (0, 1):
                dirp = rev
                memset("v", c_carry[:], 0.0, bcarry)
                chunks = range(NCH) if not rev else range(NCH - 1, -1, -1)
                for ci in chunks:
                    c0 = ci * CH
                    for dt_ in range(16):
                        xv = XU[:, dt_ * 516: dt_ * 516 + 516]
                        if not rev:
                            if c0 == 0:
                                memset("v", xv[:, 0:3], 0.0, bXU)
                                p.dma("q", xv[:, 3:515],
                                      d_XS[b, dt_ * 128:(dt_ + 1) * 128, 0:512],
                                      writes=[bXU])
                            else:
                                p.dma("q", xv[:, 0:515],
                                      d_XS[b, dt_ * 128:(dt_ + 1) * 128, c0 - 3:c0 + 512],
                                      writes=[bXU])
                        else:
                            if c0 + CH + 3 > L:
                                memset("v", xv[:, 512:515], 0.0, bXU)
                                p.dma("q", xv[:, 0:512],
                                      d_XS[b, dt_ * 128:(dt_ + 1) * 128, c0:c0 + 512],
                                      writes=[bXU])
                            else:
                                p.dma("q", xv[:, 0:515],
                                      d_XS[b, dt_ * 128:(dt_ + 1) * 128, c0:c0 + 515],
                                      writes=[bXU])
                    for dt_ in range(16):
                        xv = XU[:, dt_ * 516: dt_ * 516 + 516]
                        xc = SLOT[0]
                        for j in range(4):
                            wj = (j if not rev else 3 - j)
                            wcol = c_convw[:, dirp * 64 + dt_ * 4 + wj:
                                           dirp * 64 + dt_ * 4 + wj + 1]
                            if j == 0:
                                vts(xc, xv[:, 0:512], wcol, None, ALU.mult, ALU.bypass,
                                    bSL[0], [bXU, bconst])
                            else:
                                vstt(xc, xv[:, j:j + 512], wcol, xc, ALU.mult, ALU.add,
                                     bSL[0], [bXU, bconst, bSL[0]])
                        cbb = c_convb[:, dirp * 16 + dt_: dirp * 16 + dt_ + 1]
                        vts(xcbf[:, dt_ * 512:(dt_ + 1) * 512], xc, cbb, None,
                            ALU.add, ALU.bypass, bxcbf, [bSL[0], bconst])
                        vts(SLOT[1], xc, cbb, 10.0, ALU.add, ALU.min, bSL[1],
                            [bSL[0], bconst])
                        vts(XU[:, dt_ * 516: dt_ * 516 + 512], SLOT[1], -10.0, None,
                            ALU.max, ALU.bypass, bXU, [bSL[1]])
                    xo = dirp * 1536
                    for (m0, m1, ps_, bps) in ((0, 64, pA, bpA), (64, 80, pB, bpB),
                                               (80, 96, pC, bpC)):
                        for dt_ in range(16):
                            mm(ps_[0:m1 - m0, 0:512],
                               c_xprojT[:, xo + dt_ * 96 + m0: xo + dt_ * 96 + m1],
                               xcbf[:, dt_ * 512:(dt_ + 1) * 512],
                               dt_ == 0, dt_ == 15, bps, [bconst, bxcbf])
                    act(dtc_bf, pA[0:64, 0:512], AF.Copy, bdtc, [bpA])
                    act(Bv_bf, pB[0:16, 0:512], AF.Copy, bBv, [bpB])
                    act(Cv_bf, pC[0:16, 0:512], AF.Copy, bCv, [bpC])
                    for n in range(16):
                        mm(pT[:, 0:512], c_sel[:, n * 128:(n + 1) * 128], Cv_bf,
                           True, True, bpT, [bconst, bCv])
                        act(Cb[:, n * 512:(n + 1) * 512], pT[:, 0:512], AF.Copy,
                            bCb, [bpT])
                    for dt_ in range(16):
                        u_dt = XU[:, dt_ * 516: dt_ * 516 + 512]
                        mm(pA[:, 0:512],
                           c_dtprojT[:, dirp * 2048 + dt_ * 128: dirp * 2048 + (dt_ + 1) * 128],
                           dtc_bf, True, True, bpA, [bconst, bdtc])
                        dtb_ = c_dtb[:, dirp * 16 + dt_: dirp * 16 + dt_ + 1]
                        vts(SLOT[1], pA[:, 0:512], dtb_, None, ALU.add, ALU.bypass,
                            bSL[1], [bpA, bconst])
                        vts(SLOT[1], SLOT[1], 1e-5, 1.0, ALU.max, ALU.min, bSL[1], [bSL[1]])
                        act(SLOT[1], SLOT[1], AF.Exp, bSL[1], [bSL[1]], bias=dtb_)
                        delta = SLOT[2]
                        act(delta, SLOT[1], AF.Ln, bSL[2], [bSL[1]], bias=1.0)
                        du = SLOT[3]
                        vtt(du, delta, u_dt, ALU.mult, bSL[3], [bSL[2], bXU])
                        for ng in range(4):
                            for nl in range(4):
                                n = ng * 4 + nl
                                mm(pX[nl][:, 0:512], c_sel[:, n * 128:(n + 1) * 128],
                                   Bv_bf, True, True, bpX[nl], [bconst, bBv])
                                dA = SLOT[4] if nl % 2 == 0 else SLOT[5]
                                bdA = bSL[4] if nl % 2 == 0 else bSL[5]
                                act(dA, delta, AF.Exp, bdA, [bSL[2]],
                                    scale=c_A[:, dirp * 256 + dt_ * 16 + n:
                                              dirp * 256 + dt_ * 16 + n + 1])
                                sb = sbs[nl % 2]
                                vtt(sb, du, pX[nl][:, 0:512], ALU.mult, bsb[nl % 2],
                                    [bSL[3], bpX[nl]])
                                hsl = h_ng[:, nl * 512:(nl + 1) * 512]
                                init = c_carry[:, dt_ * 16 + n: dt_ * 16 + n + 1]
                                if not rev:
                                    vscan(hsl, dA, sb, init, bh,
                                          [bdA, bsb[nl % 2], bcarry])
                                else:
                                    vscan(hsl[:, ::-1], dA[:, ::-1], sb[:, ::-1],
                                          init, bh, [bdA, bsb[nl % 2], bcarry])
                            vtt(hC_big[:, 0:2048], h_ng[:, 0:2048],
                                Cb[:, ng * 2048:(ng + 1) * 2048], ALU.mult, bhCb,
                                [bh, bCb])
                            for nl in range(4):
                                mm(pB[:, 0:512], c_eyebf[:],
                                   hC_big[:, nl * 512:(nl + 1) * 512],
                                   (ng == 0 and nl == 0), False, bpB, [bconst, bhCb])
                            lastc = 511 if not rev else 0
                            vcopy(c_carry[:, dt_ * 16 + ng * 4: dt_ * 16 + ng * 4 + 4],
                                  h_ng[:, lastc::512], bcarry, [bh])
                        vts(uDb, u_dt, c_Dv[:, dirp * 16 + dt_: dirp * 16 + dt_ + 1],
                            None, ALU.mult, ALU.bypass, buD, [bXU, bconst])
                        mm(pB[:, 0:512], c_eyebf[:], uDb, False, True, bpB,
                           [bconst, buD])
                        y1 = SLOT[9]
                        act(y1, pB[:, 0:512], AF.Copy, bSL[9], [bpB])
                        if not rev:
                            p.dma("g", d_YF[b, dt_ * 128:(dt_ + 1) * 128, c0:c0 + 512],
                                  y1, reads=[bSL[9]])
                        else:
                            yf = SLOT[6]
                            zt = SLOT[7]
                            p.dma("q", yf, d_YF[b, dt_ * 128:(dt_ + 1) * 128, c0:c0 + 512],
                                  writes=[bSL[6]])
                            p.dma("q", zt, d_ZS[b, dt_ * 128:(dt_ + 1) * 128, c0:c0 + 512],
                                  writes=[bSL[7]])
                            vtt(y1, y1, yf, ALU.add, bSL[9], [bSL[9], bSL[6]])
                            sz = SLOT[8]
                            act(sz, zt, AF.Tanh, bSL[8], [bSL[7]], scale=c_k[:, 0:1])
                            vts(sz, sz, 0.5, 0.5, ALU.mult, ALU.add, bSL[8], [bSL[8]])
                            vtt(sz, sz, zt, ALU.mult, bSL[8], [bSL[8], bSL[7]])
                            vtt(yszb[:, dt_ * 512:(dt_ + 1) * 512], y1, sz, ALU.mult,
                                bysz, [bSL[9], bSL[8]])
                    if rev:
                        for ct in range(8):
                            op_, bop = (pA, bpA) if ct % 2 == 0 else (pC, bpC)
                            for dt_ in range(16):
                                mm(op_[:, 0:512],
                                   woutT[:, dt_ * 1024 + ct * 128: dt_ * 1024 + (ct + 1) * 128],
                                   yszb[:, dt_ * 512:(dt_ + 1) * 512],
                                   dt_ == 0, dt_ == 15, bop, [bw1, bysz])
                            o1 = SLOT[10]
                            act(o1, op_[:, 0:512], AF.Copy, bSL[10], [bop])
                            act(fin8, op_[:, 0:512], AF.Is_finite, bu8f, [bop])
                            sg = SLOT[8]
                            act(sg, op_[:, 0:512], AF.Sign, bSL[8], [bop])
                            s1 = SLOT[1]
                            p.op("v", lambda v, o=s1, m=fin8, a=o1, bb=sg:
                                 v.select(o, m, a, bb),
                                 reads=[bu8f, bSL[10], bSL[8]], writes=[bSL[1]])
                            vtt(m28, s1, s1, ALU.is_equal, bu8m, [bSL[1]])
                            s2 = SLOT[10]
                            p.op("v", lambda v, o=s2, m=m28, a=s1, bb=zero512:
                                 v.select(o, m, a, bb),
                                 reads=[bu8m, bSL[1], bzero], writes=[bSL[10]])
                            r1t = SLOT[11]
                            p.dma("q", r1t, d_R1[ct * 128:(ct + 1) * 128,
                                                 b * L + c0: b * L + c0 + 512],
                                  writes=[bSL[11]])
                            x2 = SLOT[1]
                            vtt(x2, s2, r1t, ALU.add, bSL[1], [bSL[10], bSL[11]])
                            p.dma("g", d_R2[ct * 128:(ct + 1) * 128,
                                            b * L + c0: b * L + c0 + 512], x2,
                                  reads=[bSL[1]])
                p.barrier()
        p.barrier()

        ffn_sparse(4)

        p.cmds["g"].append([lambda eng: eng.wait_ge(sems["g"], p.cnt["g"]), None])
        p.cmds["q"].append([lambda eng: eng.wait_ge(sems["q"], p.cnt["q"]), None])

        block = stk.enter_context(nc.Block())

        def replay(eng_obj, ename):
            for fn, inc in p.cmds[ename]:
                ins = fn(eng_obj)
                if inc:
                    ins.then_inc(sems[ename], inc)

        @block.tensor
        def _(t):
            replay(t, "t")

        @block.vector
        def _(v):
            replay(v, "v")

        @block.scalar
        def _(s):
            replay(s, "s")

        @block.sync
        def _(q):
            replay(q, "q")

        @block.gpsimd
        def _(g):
            replay(g, "g")

    return nc


def _prep_inputs(inp):
    g = {k: np.asarray(v) for k, v in inp.items()}
    f = np.float32
    out = {}
    out["winT"] = np.ascontiguousarray(
        g["L0_attn_in_proj_w"].T.reshape(8, 128, 4096).transpose(1, 0, 2)
        .reshape(128, 8 * 4096)).astype(f, copy=False)
    out["woutT"] = np.ascontiguousarray(
        g["L0_attn_out_proj_w"].T.reshape(16, 128, 1024).transpose(1, 0, 2)
        .reshape(128, 16 * 1024)).astype(f, copy=False)
    xp = np.stack([np.asarray(g["L0_attn_x_proj_w"]).T,
                   np.asarray(g["L0_attn_x_proj_b_w"]).T])
    out["xprojT"] = np.ascontiguousarray(
        xp.reshape(2, 16, 128, 96).transpose(2, 0, 1, 3).reshape(128, 2 * 1536)
    ).astype(f, copy=False)
    dtp = np.stack([np.asarray(g["L0_attn_dt_proj_w"]).T,
                    np.asarray(g["L0_attn_dt_proj_b_w"]).T])
    out["dtprojT"] = np.ascontiguousarray(
        dtp.transpose(1, 0, 2).reshape(64, 2 * 2048)).astype(f, copy=False)
    cw = np.stack([g["L0_attn_conv1d_w"].reshape(2048, 4),
                   g["L0_attn_conv1d_b_w"].reshape(2048, 4)])
    out["convw"] = np.ascontiguousarray(
        cw.reshape(2, 16, 128, 4).transpose(2, 0, 1, 3).reshape(128, 2 * 64)
    ).astype(f, copy=False)
    def col2(a, b):
        s = np.stack([np.asarray(a).reshape(16, 128), np.asarray(b).reshape(16, 128)])
        return np.ascontiguousarray(s.transpose(2, 0, 1).reshape(128, 32)).astype(f, copy=False)
    out["convb"] = col2(g["L0_attn_conv1d_bias"], g["L0_attn_conv1d_b_bias"])
    out["dtb"] = col2(g["L0_attn_dt_proj_bias"], g["L0_attn_dt_proj_b_bias"])
    out["Dv"] = col2(g["L0_attn_D"], g["L0_attn_D_b"])
    al = np.stack([g["L0_attn_A_log"], g["L0_attn_A_b_log"]])
    out["Alog"] = np.ascontiguousarray(
        al.reshape(2, 16, 128, 16).transpose(2, 0, 1, 3).reshape(128, 512)
    ).astype(f, copy=False)
    out["wred"] = np.ascontiguousarray(
        g["L0_ffn_w_red"].reshape(8, 128, 16).transpose(1, 0, 2).reshape(128, 128)
    ).astype(f, copy=False)
    out["wg"] = np.ascontiguousarray(g["L0_ffn_wg"]).astype(f, copy=False)
    out["W1"] = np.ascontiguousarray(g["L0_ffn_W1"]).astype(f, copy=False)
    out["W2"] = np.ascontiguousarray(g["L0_ffn_W2"]).astype(f, copy=False)
    out["b1c"] = np.ascontiguousarray(
        g["L0_ffn_b1"].reshape(8, 16, 128).transpose(2, 0, 1).reshape(128, 128)
    ).astype(f, copy=False)
    out["b2c"] = np.ascontiguousarray(
        g["L0_ffn_b2"].reshape(8, 8, 128).transpose(2, 0, 1).reshape(128, 64)
    ).astype(f, copy=False)
    out["iotap"] = np.stack([np.arange(128), 8 * np.arange(128)], axis=1).astype(f)
    out["evrow8"] = np.tile(np.arange(8).astype(f), (128, 1))
    out["u24"] = np.tile(np.arange(24).astype(f), (8, 1))
    out["eye_f32"] = np.eye(128, dtype=f)
    out["eye_bf"] = np.eye(128).astype(BFNP)
    out["ones_f32"] = np.ones((128, 128), f)
    out["ones_bf"] = np.ones((16, 512), BFNP)
    sel = np.zeros((16, 16 * 128), np.float32)
    for n in range(16):
        sel[n, n * 128:(n + 1) * 128] = 1.0
    out["sel_bf"] = sel.astype(BFNP)
    return out


def kernel(**inp):
    shared = _prep_inputs(inp)
    x = np.ascontiguousarray(
        np.asarray(inp["L0_attn_hidden_states"], dtype=np.float32).reshape(65536, 1024))
    nc = build_nc()
    in_maps = []
    for c in range(NCORE):
        m = dict(shared)
        m["x"] = np.ascontiguousarray(x[c * TOK:(c + 1) * TOK])
        in_maps.append(m)
    res = run_bass_kernel_spmd(nc, in_maps, core_ids=list(range(NCORE)))
    out = np.concatenate([np.asarray(r["out"]) for r in res.results], axis=0)
    return out.reshape(16, 4096, 1024)


if __name__ == "__main__":
    build_nc()
    print("built ok")


# version 11
# speedup vs baseline: 1.1157x; 1.1157x over previous
import numpy as np
import ml_dtypes
import concourse.bass as bass
import concourse.mybir as mybir
from concourse.bass_utils import run_bass_kernel_spmd
from contextlib import ExitStack

F32 = mybir.dt.float32
BF16 = mybir.dt.bfloat16
U8 = mybir.dt.uint8
AF = mybir.ActivationFunctionType
ALU = mybir.AluOpType
AX = mybir.AxisListType
BFNP = ml_dtypes.bfloat16

NCORE = 8
B_LOC = 2
L = 4096
CDIM = 1024
DIN = 2048
TOK = B_LOC * L
ST = 512
NSUP = TOK // ST
CH = 512
NCH = L // CH
UCAP = 23
NTT = TOK // 128

DEBUG = False
T_SCAT = True
T_WG = True
T_BG = True
T_UNS = True

ENGINES = ("t", "v", "s", "q", "g")


class Buf:
    __slots__ = ("w", "readers", "name")
    def __init__(self, name):
        self.name = name
        self.w = None
        self.readers = set()


class Prog:
    def __init__(self, nc, sems):
        self.nc = nc
        self.sems = sems
        self.cmds = {e: [] for e in ENGINES}
        self.cnt = {e: 0 for e in ENGINES}
        self.lastidx = {e: -1 for e in ENGINES}
        self.markidx = {e: -1 for e in ENGINES}
        self.waited = {}

    def _emit(self, e, fn):
        self.cmds[e].append([fn, 0])
        self.lastidx[e] = len(self.cmds[e]) - 1
        return self.lastidx[e]

    def mark(self, e):
        li = self.lastidx[e]
        if li < 0:
            return (e, 0)
        if self.markidx[e] == li:
            return (e, self.cnt[e])
        inc = 16 if e in ("q", "g") else 1
        self.cmds[e][li][1] += inc
        self.cnt[e] += inc
        self.markidx[e] = li
        return (e, self.cnt[e])

    def wait(self, consumer, evt):
        pe, val = evt
        if val <= 0 or pe == consumer:
            return
        key = (consumer, pe)
        if self.waited.get(key, 0) >= val:
            return
        self.waited[key] = val
        sem = self.sems[pe]
        self.cmds[consumer].append(
            [lambda eng, s=sem, v=val: eng.wait_ge(s, v), None])

    def _hazards(self, e, reads, writes):
        for b in reads:
            if b.w is not None and b.w != e:
                self.wait(e, self.mark(b.w))
        for b in writes:
            if b.w is not None and b.w != e:
                self.wait(e, self.mark(b.w))
            for r in list(b.readers):
                if r != e:
                    self.wait(e, self.mark(r))
        for b in reads:
            b.readers.add(e)
        for b in writes:
            b.w = e
            b.readers = set()

    def op(self, e, fn, reads=(), writes=()):
        self._hazards(e, reads, writes)
        return self._emit(e, fn)

    def dma(self, e, out, in_, reads=(), writes=()):
        assert e in ("q", "g")
        self._hazards(e, reads, writes)
        idx = self._emit(e, lambda eng, o=out, i=in_: eng.dma_start(out=o, in_=i))
        self.cmds[e][idx][1] = 16
        self.cnt[e] += 16
        self.markidx[e] = idx

    def dma_fn(self, e, fn, reads=(), writes=()):
        self._hazards(e, reads, writes)
        idx = self._emit(e, fn)
        self.cmds[e][idx][1] = 16
        self.cnt[e] += 16
        self.markidx[e] = idx

    def barrier(self):
        evts = [self.mark(e) for e in ENGINES]
        for c in ENGINES:
            for evt in evts:
                self.wait(c, evt)


def build_nc():
    nc = bass.Bass("TRN2", target_bir_lowering=False)
    scratch_kind = "ExternalOutput" if DEBUG else "Internal"

    d_x = nc.dram_tensor("x", [TOK, CDIM], F32, kind="ExternalInput").ap()
    d_out = nc.dram_tensor("out", [TOK, CDIM], F32, kind="ExternalOutput").ap()

    d_winT = nc.dram_tensor("winT", [128, 8 * 4096], F32, kind="ExternalInput").ap()
    d_woutT = nc.dram_tensor("woutT", [128, 16 * 1024], F32, kind="ExternalInput").ap()
    d_xprojT = nc.dram_tensor("xprojT", [128, 2 * 1536], F32, kind="ExternalInput").ap()
    d_dtprojT = nc.dram_tensor("dtprojT", [64, 2 * 2048], F32, kind="ExternalInput").ap()
    d_convw = nc.dram_tensor("convw", [128, 2 * 64], F32, kind="ExternalInput").ap()
    d_convb = nc.dram_tensor("convb", [128, 2 * 16], F32, kind="ExternalInput").ap()
    d_dtb = nc.dram_tensor("dtb", [128, 2 * 16], F32, kind="ExternalInput").ap()
    d_Alog = nc.dram_tensor("Alog", [128, 2 * 256], F32, kind="ExternalInput").ap()
    d_Dv = nc.dram_tensor("Dv", [128, 2 * 16], F32, kind="ExternalInput").ap()
    d_wred = nc.dram_tensor("wred", [128, 128], F32, kind="ExternalInput").ap()
    d_wg = nc.dram_tensor("wg", [8, 16], F32, kind="ExternalInput").ap()
    d_W1 = nc.dram_tensor("W1", [8, 1024, 2048], F32, kind="ExternalInput").ap()
    d_b1c = nc.dram_tensor("b1c", [128, 128], F32, kind="ExternalInput").ap()
    d_W2 = nc.dram_tensor("W2", [8, 2048, 1024], F32, kind="ExternalInput").ap()
    d_b2c = nc.dram_tensor("b2c", [128, 64], F32, kind="ExternalInput").ap()
    d_iotap = nc.dram_tensor("iotap", [128, 2], F32, kind="ExternalInput").ap()
    d_evrow8 = nc.dram_tensor("evrow8", [128, 8], F32, kind="ExternalInput").ap()
    d_u24 = nc.dram_tensor("u24", [8, 24], F32, kind="ExternalInput").ap()
    d_eye = nc.dram_tensor("eye_f32", [128, 128], F32, kind="ExternalInput").ap()
    d_eyebf = nc.dram_tensor("eye_bf", [128, 128], BF16, kind="ExternalInput").ap()
    d_ones = nc.dram_tensor("ones_f32", [128, 128], F32, kind="ExternalInput").ap()
    d_onesbf = nc.dram_tensor("ones_bf", [16, 512], BF16, kind="ExternalInput").ap()
    d_sel = nc.dram_tensor("sel_bf", [16, 16 * 128], BF16, kind="ExternalInput").ap()

    if DEBUG:
        d_DBG = nc.dram_tensor("DBG", [16, 128, 512], F32, kind="ExternalOutput").ap()
        d_DBGU8 = nc.dram_tensor("DBGU8", [128, 8192], U8, kind="ExternalOutput").ap()
        d_DBGBF = nc.dram_tensor("DBGBF", [8, 128, 512], BF16, kind="ExternalOutput").ap()
    else:
        d_DBG = d_DBGU8 = d_DBGBF = None
    d_R1 = nc.dram_tensor("R1", [CDIM, TOK], F32, kind=scratch_kind).ap()
    d_R2 = nc.dram_tensor("R2", [CDIM, TOK], F32, kind=scratch_kind).ap()
    d_XS = nc.dram_tensor("XS", [B_LOC, DIN, L], F32, kind=scratch_kind).ap()
    d_ZS = nc.dram_tensor("ZS", [B_LOC, DIN, L], F32, kind=scratch_kind).ap()
    d_YF = nc.dram_tensor("YF", [B_LOC, DIN, L], F32, kind=scratch_kind).ap()
    d_W1B = nc.dram_tensor("W1B", [1024, 16384], BF16, kind="Internal").ap()
    d_W2B = nc.dram_tensor("W2B", [1024, 16384], BF16, kind="Internal").ap()
    d_XSRT = nc.dram_tensor("XSRT", [UCAP * ST, CDIM], F32, kind=scratch_kind).ap()
    d_OSRT = nc.dram_tensor("OSRT", [UCAP * ST, CDIM], F32, kind=scratch_kind).ap()
    d_X3 = nc.dram_tensor("X3", [TOK, CDIM], F32, kind=scratch_kind).ap()

    with ExitStack() as stk:
        def sbuf(name, shape, dt):
            return stk.enter_context(nc.sbuf_tensor(name, shape, dt))
        def psum(name, shape, dt):
            return stk.enter_context(nc.psum_tensor(name, shape, dt))

        AF8 = sbuf("AF8", [128, 14400], F32)
        AF8bf = AF8.bitcast(BF16)
        BF8 = sbuf("BF8", [128, 34816], BF16)
        BF2a = sbuf("BF2a", [128, 8192], BF16)
        BF2b = sbuf("BF2b", [128, 2048], BF16)
        U8a = sbuf("U8a", [128, 1024], U8)
        SM = sbuf("SM", [128, 2560], F32)
        BFS = sbuf("BFS", [128, 4096], BF16)
        I32 = mybir.dt.int32
        g_eid = sbuf("g_eid", [128, 64], F32)
        g_rank = sbuf("g_rank", [128, 64], F32)
        g_ssb = sbuf("g_ssb", [128, 64], F32)
        g_dest = sbuf("g_dest", [128, 64], F32)
        g_desti = sbuf("g_desti", [128, 64], I32)
        g_scale = sbuf("g_scale", [128, 64], F32)
        g_expu = sbuf("g_expu", [128, 24], F32)
        g_w1i = sbuf("g_w1i", [128, 24], I32)
        g_w2i = sbuf("g_w2i", [128, 24], I32)
        g_bii = sbuf("g_bii", [128, 24], I32)
        g_vb = sbuf("g_vb", [128, 8], F32)
        g_s8 = sbuf("g_s8", [8, 128], F32)
        c_one8 = sbuf("c_one8", [8, 512], F32)
        c_iotap = sbuf("c_iotap", [128, 2], F32)
        c_evrow8 = sbuf("c_evrow8", [128, 8], F32)
        c_u24 = sbuf("c_u24", [8, 24], F32)
        c_b1u = sbuf("c_b1u", [128, 16], F32)
        c_b2u = sbuf("c_b2u", [128, 8], F32)
        c_b1all = sbuf("c_b1all", [128, 128], F32)
        c_b2all = sbuf("c_b2all", [128, 64], F32)
        c_ind = sbuf("c_ind", [128, 8], F32)
        c_sp = sbuf("c_sp", [128, 32], F32)
        c_z = sbuf("c_z", [128, 1], F32)
        c_eye = sbuf("c_eye", [128, 128], F32)
        c_eyebf = sbuf("c_eyebf", [128, 128], BF16)
        c_ones = sbuf("c_ones", [128, 128], F32)
        c_onesbf = sbuf("c_onesbf", [16, 512], BF16)
        c_sel = sbuf("c_sel", [16, 16 * 128], BF16)
        c_wred = sbuf("c_wred", [128, 128], F32)
        c_convw = sbuf("c_convw", [128, 128], F32)
        c_convb = sbuf("c_convb", [128, 32], F32)
        c_dtb = sbuf("c_dtb", [128, 32], F32)
        c_Dv = sbuf("c_Dv", [128, 32], F32)
        c_A = sbuf("c_A", [128, 512], F32)
        c_xprojT = sbuf("c_xprojT", [128, 2 * 1536], BF16)
        c_dtprojT = sbuf("c_dtprojT", [64, 2 * 2048], BF16)
        c_wgn = sbuf("c_wgn", [8, 16], F32)
        c_wgnT = sbuf("c_wgnT", [16, 8], F32)
        c_carry = sbuf("c_carry", [128, 256], F32)
        c_k = sbuf("c_k", [128, 4], F32)

        pA = psum("pA", [128, 512], F32)
        pB = psum("pB", [128, 512], F32)
        pC = psum("pC", [128, 512], F32)
        pX = [psum(f"pX{i}", [128, 512], F32) for i in range(4)]
        pT = psum("pT", [128, 512], F32)

        sems = {e: stk.enter_context(nc.semaphore(f"sem_{e}")) for e in ENGINES}
        p = Prog(nc, sems)

        bufs = {}
        def B_(name):
            if name not in bufs:
                bufs[name] = Buf(name)
            return bufs[name]
        bconst = B_("const")
        bSM = B_("SM")
        bpA, bpB, bpC, bpT = B_("pA"), B_("pB"), B_("pC"), B_("pT")
        bpX = [B_(f"pX{i}") for i in range(4)]
        bcarry = B_("carry")

        def mm(ps_ap, lhsT, rhs, start, stop, wps, rds):
            p.op("t", lambda t, o=ps_ap, l=lhsT, r=rhs, a=start, b=stop:
                 t.matmul(o, l, r, start=a, stop=b), reads=rds, writes=[wps])

        def tr(ps_ap, in_ap, ident, wps, rds):
            p.op("t", lambda t, o=ps_ap, i=in_ap, e=ident:
                 t.transpose(o, i, e), reads=rds, writes=[wps])

        def act(out_ap, in_ap, func, wbuf, rds, bias=0.0, scale=1.0):
            p.op("s", lambda s, o=out_ap, i=in_ap, f=func, b=bias, sc=scale:
                 s.activation(o, i, f, bias=b, scale=sc), reads=rds, writes=[wbuf])

        def vcopy(out_ap, in_ap, wbuf, rds):
            p.op("v", lambda v, o=out_ap, i=in_ap: v.tensor_copy(o, i),
                 reads=rds, writes=[wbuf])

        def vtt(out_ap, a_ap, b_ap, alu, wbuf, rds):
            p.op("v", lambda v, o=out_ap, a=a_ap, b=b_ap, u=alu:
                 v.tensor_tensor(o, a, b, u), reads=rds, writes=[wbuf])

        def vts(out_ap, in_ap, s1, s2, op0, op1, wbuf, rds):
            if s2 is None:
                p.op("v", lambda v, o=out_ap, i=in_ap, a=s1, u0=op0:
                     v.tensor_scalar(o, i, a, None, u0), reads=rds, writes=[wbuf])
            else:
                p.op("v", lambda v, o=out_ap, i=in_ap, a=s1, b=s2, u0=op0, u1=op1:
                     v.tensor_scalar(o, i, a, b, u0, u1), reads=rds, writes=[wbuf])

        def vstt(out_ap, in0, scal, in1, op0, op1, wbuf, rds):
            p.op("v", lambda v, o=out_ap, i0=in0, sc=scal, i1=in1, u0=op0, u1=op1:
                 v.scalar_tensor_tensor(o, i0, sc, i1, u0, u1), reads=rds, writes=[wbuf])

        def gts(out_ap, in_ap, s1, op0, wbuf, rds):
            p.op("g", lambda g, o=out_ap, i=in_ap, a=s1, u0=op0:
                 g.tensor_scalar(o, i, a, None, u0), reads=rds, writes=[wbuf])

        def gtt(out_ap, a_ap, b_ap, alu, wbuf, rds):
            p.op("g", lambda g, o=out_ap, a=a_ap, b=b_ap, u=alu:
                 g.tensor_tensor(o, a, b, u), reads=rds, writes=[wbuf])

        def vscan(out_ap, d0, d1, init, wbuf, rds):
            p.op("v", lambda v, o=out_ap, a=d0, b=d1, i=init:
                 v.tensor_tensor_scan(o, a, b, i, ALU.mult, ALU.add),
                 reads=rds, writes=[wbuf])

        def vred(out_ap, in_ap, alu_max, wbuf, rds):
            if alu_max:
                p.op("v", lambda v, o=out_ap, i=in_ap:
                     v.reduce_max(o, i, axis=AX.X), reads=rds, writes=[wbuf])
            else:
                p.op("v", lambda v, o=out_ap, i=in_ap:
                     v.reduce_sum(o, i, axis=AX.X), reads=rds, writes=[wbuf])

        def memset(e, ap, val, wbuf):
            p.op(e, lambda eng, a=ap, c=val: eng.memset(a, c), writes=[wbuf])

        for dst, src in [
            (c_eye[:], d_eye), (c_eyebf[:], d_eyebf), (c_ones[:], d_ones),
            (c_onesbf[:], d_onesbf), (c_sel[:], d_sel),
            (c_wred[:], d_wred), (c_convw[:], d_convw),
            (c_convb[:], d_convb), (c_dtb[:], d_dtb), (c_Dv[:], d_Dv),
            (c_iotap[:], d_iotap), (c_evrow8[:], d_evrow8), (c_u24[:], d_u24),
            (c_b1all[:], d_b1c), (c_b2all[:], d_b2c),
        ]:
            p.dma("q", dst, src, writes=[bconst])
        memset("v", c_k[:, 0:1], 0.5, bconst)
        memset("v", c_k[:, 1:2], -0.5, bconst)
        memset("v", c_k[:, 2:3], -1.0, bconst)
        memset("v", c_k[:, 3:4], -0.6931471805599453, bconst)
        memset("v", c_one8[:], 1.0, bconst)
        memset("v", c_z[:], 0.0, bconst)
        p.dma("q", SM[:, 0:512], d_Alog, writes=[bSM])
        act(c_A[:], SM[:, 0:512], AF.Exp, bconst, [bSM])
        vts(c_A[:], c_A[:], -1.0, None, ALU.mult, ALU.bypass, bconst, [bconst])
        p.dma("q", SM[:, 0:1536], d_xprojT[:, 0:1536], writes=[bSM])
        vcopy(c_xprojT[:, 0:1536], SM[:, 0:1536], bconst, [bSM])
        p.dma("q", SM[:, 0:1536], d_xprojT[:, 1536:3072], writes=[bSM])
        vcopy(c_xprojT[:, 1536:3072], SM[:, 0:1536], bconst, [bSM])
        p.dma("q", SM[0:64, 0:2048], d_dtprojT[:, 0:2048], writes=[bSM])
        vcopy(c_dtprojT[:, 0:2048], SM[0:64, 0:2048], bconst, [bSM])
        p.dma("q", SM[0:64, 0:2048], d_dtprojT[:, 2048:4096], writes=[bSM])
        vcopy(c_dtprojT[:, 2048:4096], SM[0:64, 0:2048], bconst, [bSM])
        p.dma("q", c_wgn[:], d_wg, writes=[bconst])
        act(SM[0:8, 0:16], c_wgn[:], AF.Square, bSM, [bconst])
        vred(SM[0:8, 16:17], SM[0:8, 0:16], False, bSM, [bSM])
        act(SM[0:8, 17:18], SM[0:8, 16:17], AF.Sqrt, bSM, [bSM])
        vts(SM[0:8, 18:19], SM[0:8, 17:18], 1e-4, None, ALU.max, ALU.bypass, bSM, [bSM])
        act(SM[0:8, 21:22], SM[0:8, 18:19], AF.Ln, bSM, [bSM])
        act(SM[0:8, 22:23], SM[0:8, 0:1], AF.Square, bSM, [bSM])
        act(SM[0:8, 23:24], SM[0:8, 0:1], AF.Square, bSM, [bSM])
        act(SM[0:8, 19:20], SM[0:8, 21:22], AF.Exp, bSM, [bSM], scale=c_k[0:8, 2:3])
        vts(c_wgn[:], c_wgn[:], SM[0:8, 19:20], None, ALU.mult, ALU.bypass,
            bconst, [bconst, bSM])
        mm(pT[0:16, 0:8], c_wgn[:], c_eye[0:8, 0:8], True, True, bpT, [bconst])
        act(c_wgnT[:], pT[0:16, 0:8], AF.Copy, bconst, [bpT])

        bwc = [B_("wc0"), B_("wc1")]
        bwcb = [B_("wcb0"), B_("wcb1")]
        for src, dst, rows, w in ((d_W1, d_W1B, 1024, 2048), (d_W2, d_W2B, 2048, 1024)):
            for e in range(8):
                for tile in range(rows // 128):
                    i = tile % 2
                    sl = AF8[:, 4096 + i * w: 4096 + (i + 1) * w]
                    bfl = BFS[:, i * w:(i + 1) * w]
                    p.dma("q", sl, src[e, tile * 128:(tile + 1) * 128, :], writes=[bwc[i]])
                    vcopy(bfl, sl, bwcb[i], [bwc[i]])
                    p.dma("g", dst[e * 128:(e + 1) * 128, tile * w:(tile + 1) * w],
                          bfl, reads=[bwcb[i]])

        bxT = B_("xT")
        boacc = B_("oacc")
        bgh = B_("gh")
        blnbf = B_("lnbf")
        bmask = B_("mask")
        bw1 = B_("w1")
        bw2 = B_("w2")

        def ln_norm(get_x, routing, lnbf, bk1=None, bk2=None):
            pA_, bpA_ = bk1 if bk1 is not None else (pA, bpA)
            pB_, bpB_ = bk2 if bk2 is not None else (pB, bpB)
            for ct in range(8):
                xct, bx = get_x(ct)
                act(SM[:, 0:512], xct, AF.Square, bSM, [bx])
                mm(pA_[0:1, 0:512], c_ones[:, 0:1], xct, ct == 0, ct == 7, bpA_,
                   [bx, bconst])
                mm(pB_[0:1, 0:512], c_ones[:, 0:1], SM[:, 0:512], ct == 0, ct == 7,
                   bpB_, [bSM, bconst])
            A_ = SM[0:1, 1024:1536]
            Bs = SM[0:1, 1536:2048]
            Cs = SM[0:1, 2048:2560]
            act(A_, pA_[0:1, 0:512], AF.Copy, bSM, [bpA_])
            act(Bs, pB_[0:1, 0:512], AF.Copy, bSM, [bpB_])
            vts(A_, A_, 1.0 / CDIM, None, ALU.mult, ALU.bypass, bSM, [bSM])
            vts(Bs, Bs, 1.0 / CDIM, None, ALU.mult, ALU.bypass, bSM, [bSM])
            vtt(Cs, A_, A_, ALU.mult, bSM, [bSM])
            vtt(Bs, Bs, Cs, ALU.subtract, bSM, [bSM])
            vts(Bs, Bs, 1e-5, None, ALU.add, ALU.bypass, bSM, [bSM])
            act(Cs, Bs, AF.Ln, bSM, [bSM])
            act(Bs, Cs, AF.Exp, bSM, [bSM], scale=c_k[0:1, 1:2])
            vtt(Cs, A_, Bs, ALU.mult, bSM, [bSM])
            mm(pA_[:, 0:512], c_ones[0:1, :], Bs, True, True, bpA_, [bSM, bconst])
            mm(pB_[:, 0:512], c_ones[0:1, :], Cs, True, True, bpB_, [bSM, bconst])
            for ct in range(8):
                xct, bx = get_x(ct)
                t1 = SM[:, 0:512]
                vtt(t1, xct, pA_[:, 0:512], ALU.mult, bSM, [bx, bpA_])
                lnf = SM[:, 512:1024]
                vtt(lnf, t1, pB_[:, 0:512], ALU.subtract, bSM, [bSM, bpB_])
                vcopy(lnbf[:, ct * 512:(ct + 1) * 512], lnf, blnbf, [bSM])
                if routing:
                    mm(pC[0:16, 0:512], c_wred[:, ct * 16:(ct + 1) * 16], lnf,
                       ct == 0, ct == 7, bpC, [bSM, bconst])

        bG = B_("G")

        def routing_A(sup):
            xred = SM[0:16, 0:512]
            act(xred, pC[0:16, 0:512], AF.Copy, bSM, [bpC])
            mm(pC[0:8, 0:512], c_wgnT[:], xred, True, True, bpC, [bSM, bconst])
            lg = SM[0:8, 512:1024]
            act(lg, pC[0:8, 0:512], AF.Copy, bSM, [bpC])
            ltok = SM[:, 0:32]
            for tb in range(4):
                tr(pT[0:128, 0:8], lg[:, tb * 128:(tb + 1) * 128], c_eye[0:8, 0:8],
                   bpT, [bSM])
                act(ltok[:, tb * 8:(tb + 1) * 8], pT[0:128, 0:8], AF.Copy, bSM, [bpT])
            lmax = SM[:, 32:36]
            nmax = SM[:, 36:40]
            ee = SM[:, 40:72]
            ssum = SM[:, 72:76]
            maskf = SM[:, 80:112]
            cumf = SM[:, 112:144]
            eidt = SM[:, 144:176]
            for tb in range(4):
                lt = ltok[:, tb * 8:(tb + 1) * 8]
                vred(lmax[:, tb:tb + 1], lt, True, bSM, [bSM])
                p.op("v", lambda v, o=nmax[:, tb:tb + 1], i=lt:
                     v.reduce_max(o, i, axis=AX.X, negate=True),
                     reads=[bSM], writes=[bSM])
            for tb in range(4):
                lt = ltok[:, tb * 8:(tb + 1) * 8]
                act(ee[:, tb * 8:(tb + 1) * 8], lt, AF.Exp, bSM, [bSM],
                    bias=nmax[:, tb:tb + 1])
                vts(maskf[:, tb * 8:(tb + 1) * 8], lt, lmax[:, tb:tb + 1], None,
                    ALU.is_equal, ALU.bypass, bSM, [bSM])
            for tb in range(4):
                vred(ssum[:, tb:tb + 1], ee[:, tb * 8:(tb + 1) * 8], False, bSM, [bSM])
            for tb in range(4):
                vscan(cumf[:, tb * 8:(tb + 1) * 8], c_ones[:, 0:8],
                      maskf[:, tb * 8:(tb + 1) * 8], c_z[:, 0:1], bSM, [bSM, bconst])
            for tb in range(4):
                vts(eidt[:, tb * 8:(tb + 1) * 8], cumf[:, tb * 8:(tb + 1) * 8], 1.0,
                    None, ALU.is_equal, ALU.bypass, bSM, [bSM])
            for tb in range(4):
                mf = maskf[:, tb * 8:(tb + 1) * 8]
                vtt(mf, mf, eidt[:, tb * 8:(tb + 1) * 8], ALU.mult, bSM, [bSM])
            maskT = SM[0:8, 1024:1536]
            for tb in range(4):
                tr(pT[0:8, 0:128], maskf[:, tb * 8:(tb + 1) * 8], c_eye[:], bpT, [bSM])
                act(maskT[:, tb * 128:(tb + 1) * 128], pT[0:8, 0:128], AF.Copy, bSM, [bpT])
            cumT = SM[0:8, 2048:2560]
            vscan(cumT, c_one8[0:8, 0:512], maskT, g_s8[:, 0:1], bSM, [bSM, bconst, bG])
            for tb in range(4):
                tr(pT[0:128, 0:8], cumT[:, tb * 128:(tb + 1) * 128], c_eye[0:8, 0:8],
                   bpT, [bSM])
                act(cumf[:, tb * 8:(tb + 1) * 8], pT[0:128, 0:8], AF.Copy, bSM, [bpT])
            vcopy(g_s8[:, 0:1], cumT[:, 511:512], bG, [bSM])
            rkt = SM[:, 176:208]
            for tb in range(4):
                mf = maskf[:, tb * 8:(tb + 1) * 8]
                vtt(eidt[:, tb * 8:(tb + 1) * 8], mf, c_evrow8[:], ALU.mult, bSM,
                    [bSM, bconst])
                vtt(rkt[:, tb * 8:(tb + 1) * 8], mf, cumf[:, tb * 8:(tb + 1) * 8],
                    ALU.mult, bSM, [bSM])
            for tb in range(4):
                j = sup * 4 + tb
                vred(g_eid[:, j:j + 1], eidt[:, tb * 8:(tb + 1) * 8], False, bG, [bSM])
                vred(g_rank[:, j:j + 1], rkt[:, tb * 8:(tb + 1) * 8], False, bG, [bSM])
                vcopy(g_ssb[:, j:j + 1], ssum[:, tb:tb + 1], bG, [bSM])

        def vspace(n=4):
            for i in range(n):
                vcopy(c_sp[:, (i % 4) * 8:(i % 4) * 8 + 8], c_ones[:, 0:8],
                      B_("sp"), [bconst])

        def unit_plan():
            s8 = g_s8
            vts(s8[:, 1:2], s8[:, 0:1], 511.0, 1.0 / 512.0, ALU.add, ALU.mult, bG, [bG])
            vspace()
            memset("v", s8[:, 96:112], 0.0, bG)
            for k in range(1, 17):
                vts(s8[:, 95 + k:96 + k], s8[:, 1:2], float(k), None, ALU.is_ge,
                    ALU.bypass, bG, [bG])
            vspace()
            vred(s8[:, 2:3], s8[:, 96:112], False, bG, [bG])
            tr(pT[0:1, 0:8], s8[:, 2:3], c_eye[0:8, 0:8], bpT, [bG])
            act(s8[0:1, 8:16], pT[0:1, 0:8], AF.Copy, bG, [bpT])
            vscan(s8[0:1, 16:24], c_one8[0:1, 0:8], s8[0:1, 8:16], c_z[0:1, 0:1],
                  bG, [bG, bconst])
            vspace()
            vtt(s8[0:1, 24:32], s8[0:1, 16:24], s8[0:1, 8:16], ALU.subtract, bG, [bG])
            tr(pT[0:8, 0:1], s8[0:1, 24:32], c_eye[0:1, 0:1], bpT, [bG])
            act(s8[:, 4:5], pT[0:8, 0:1], AF.Copy, bG, [bpT])
            vtt(s8[:, 5:6], s8[:, 4:5], s8[:, 2:3], ALU.add, bG, [bG])
            vts(s8[:, 32:56], c_u24[:], s8[:, 4:5], None, ALU.is_ge, ALU.bypass,
                bG, [bconst, bG])
            vspace()
            vts(s8[:, 56:80], c_u24[:], s8[:, 5:6], None, ALU.is_lt, ALU.bypass,
                bG, [bconst, bG])
            vspace()
            vtt(s8[:, 32:56], s8[:, 32:56], s8[:, 56:80], ALU.mult, bG, [bG])
            vspace()
            vts(s8[:, 56:80], s8[:, 32:56], c_iotap[0:8, 0:1], None, ALU.mult,
                ALU.bypass, bG, [bG, bconst])
            vspace()
            mm(pT[0:1, 32:56], c_one8[:, 0:1], s8[:, 56:80], True, True, bpT,
               [bG, bconst])
            act(s8[0:1, 104:128], pT[0:1, 32:56], AF.Copy, bG, [bpT])
            mm(pT[:, 0:24], c_ones[0:1, :], s8[0:1, 104:128], True, True, bpT,
               [bG, bconst])
            act(g_expu[:], pT[:, 0:24], AF.Copy, bG, [bpT])
            mm(pT[:, 32:40], c_ones[0:1, :], s8[0:1, 24:32], True, True, bpT,
               [bG, bconst])
            act(g_vb[:], pT[:, 32:40], AF.Copy, bG, [bpT])
            memset("v", g_dest[:], 0.0, bG)
            for e in range(8):
                vts(SM[:, e * 64:(e + 1) * 64], g_eid[:], float(e), None,
                    ALU.is_equal, ALU.bypass, bSM, [bG])
            for e in range(8):
                vstt(g_dest[:], SM[:, e * 64:(e + 1) * 64], g_vb[:, e:e + 1],
                     g_dest[:], ALU.mult, ALU.add, bG, [bSM, bG])
            vspace()
            vts(g_dest[:], g_dest[:], 512.0, -1.0, ALU.mult, ALU.add, bG, [bG])
            vspace()
            vtt(g_dest[:], g_dest[:], g_rank[:], ALU.add, bG, [bG])
            vspace()
            vts(g_dest[:], g_dest[:], 0.0, float(UCAP * ST - 1), ALU.max, ALU.min,
                bG, [bG])
            vspace()
            vcopy(g_desti[:], g_dest[:], bG, [bG])
            act(SM[:, 512:576], g_ssb[:], AF.Ln, bSM, [bG])
            act(c_sp[:, 0:8], c_ones[:, 0:8], AF.Copy, B_("sp"), [bconst])
            act(c_sp[:, 8:16], c_ones[:, 0:8], AF.Copy, B_("sp"), [bconst])
            act(g_scale[:], SM[:, 512:576], AF.Exp, bG, [bSM], scale=c_k[:, 2:3],
                bias=c_k[:, 3:4])
            specs = ((128.0, 0, g_w1i, 1023.0),)
            for i, (mulc, iot, dst, hi) in enumerate(specs):
                vts(SM[:, i * 32:i * 32 + 24], g_expu[:], mulc, None, ALU.mult,
                    ALU.bypass, bSM, [bG])
            vspace()
            for i, (mulc, iot, dst, hi) in enumerate(specs):
                vts(SM[:, i * 32:i * 32 + 24], SM[:, i * 32:i * 32 + 24],
                    c_iotap[:, iot:iot + 1], None, ALU.add, ALU.bypass, bSM,
                    [bSM, bconst])
            vspace()
            for i, (mulc, iot, dst, hi) in enumerate(specs):
                vts(SM[:, i * 32:i * 32 + 24], SM[:, i * 32:i * 32 + 24], 0.0, hi,
                    ALU.max, ALU.min, bSM, [bSM])
            vspace()
            for i, (mulc, iot, dst, hi) in enumerate(specs):
                vcopy(dst[:], SM[:, i * 32:i * 32 + 24], bG, [bSM])

        xT = AF8[:, 0:4096]
        o_f32 = AF8[:, 4096:8192]
        ght = AF8bf[:, 16384:24576]
        xrs = [AF8[:, 12288:13312], AF8[:, 13312:14336]]
        bxr = [B_("xr0"), B_("xr1")]
        xos = [AF8[:, 0:1024], AF8[:, 1024:2048]]
        bxo = [B_("xo0"), B_("xo1")]
        w1buf = BF8[:, 0:16384]
        w2buf = BF8[:, 16384:32768]
        bw1s = B_("w1s")
        bw2s = B_("w2s")
        IOOA = bass.IndirectOffsetOnAxis

        def gather_w(u, which):
            buf, bb, src = ((w1buf, bw1s, d_W1B) if which == 1 else
                            (w2buf, bw2s, d_W2B))
            p.dma_fn("g", lambda g, o=buf, ix=g_w1i[:, u:u + 1], s=src:
                     g.indirect_dma_start(out=o, out_offset=None, in_=s,
                                          in_offset=IOOA(ix, 0), element_offset=0,
                                          bounds_check=None, oob_is_err=False),
                     reads=[bG], writes=[bb])

        def gather_b(u):
            for e in range(8):
                vts(c_ind[:, e:e + 1], g_expu[:, u:u + 1], float(e), None,
                    ALU.is_equal, ALU.bypass, B_("ind"), [bG])
            for e in range(8):
                if e == 0:
                    vts(c_b1u[:], c_b1all[:, 0:16], c_ind[:, 0:1], None,
                        ALU.mult, ALU.bypass, B_("b1u"), [bconst, B_("ind")])
                    vts(c_b2u[:], c_b2all[:, 0:8], c_ind[:, 0:1], None,
                        ALU.mult, ALU.bypass, B_("b2u"), [bconst, B_("ind")])
                else:
                    vstt(c_b1u[:], c_b1all[:, e * 16:(e + 1) * 16], c_ind[:, e:e + 1],
                         c_b1u[:], ALU.mult, ALU.add, B_("b1u"),
                         [bconst, B_("ind"), B_("b1u")])
                    vstt(c_b2u[:], c_b2all[:, e * 8:(e + 1) * 8], c_ind[:, e:e + 1],
                         c_b2u[:], ALU.mult, ALU.add, B_("b2u"),
                         [bconst, B_("ind"), B_("b2u")])

        def p1_src(sup):
            for tk in range(4):
                xtile = SM[:, 0:1024]
                p.dma("q", xtile, d_x[sup * ST + tk * 128: sup * ST + (tk + 1) * 128, :],
                      writes=[bSM])
                for cb in range(8):
                    tr(pT[:, 0:128], xtile[:, cb * 128:(cb + 1) * 128], c_eye[:],
                       bpT, [bSM])
                    act(xT[:, cb * 512 + tk * 128: cb * 512 + (tk + 1) * 128],
                        pT[:, 0:128], AF.Copy, bxT, [bpT])
            def get(ct):
                return xT[:, ct * 512:(ct + 1) * 512], bxT
            return get

        def p4A_src(sup):
            for ct in range(8):
                p.dma("q", xT[:, ct * 512:(ct + 1) * 512],
                      d_R2[ct * 128:(ct + 1) * 128, sup * ST:(sup + 1) * ST],
                      writes=[bxT])
            for tk in range(4):
                xr = xrs[tk % 2]
                for cb in range(8):
                    tr(pT[:, 0:128], xT[:, cb * 512 + tk * 128: cb * 512 + (tk + 1) * 128],
                       c_eye[:], bpT, [bxT])
                    act(xr[:, cb * 128:(cb + 1) * 128], pT[:, 0:128], AF.Copy,
                        bxr[tk % 2], [bpT])
                p.dma("g", d_X3[(sup * 4 + tk) * 128:(sup * 4 + tk + 1) * 128, :], xr,
                      reads=[bxr[tk % 2]], writes=[B_("dX3")])
            def get(ct):
                return xT[:, ct * 512:(ct + 1) * 512], bxT
            return get

        def pass_C(src_tm):
            for j in range(NTT):
                xr = xrs[j % 2]
                p.dma("q", xr, src_tm[j * 128:(j + 1) * 128, :], writes=[bxr[j % 2]],
                      reads=[B_("dX3")] if src_tm is d_X3 else [])
                if T_SCAT:
                    p.dma_fn("g", lambda g, i=xr, ix=g_desti[:, j:j + 1]:
                             g.indirect_dma_start(out=d_XSRT, out_offset=IOOA(ix, 0),
                                                  in_=i, in_offset=None,
                                                  element_offset=0, bounds_check=None,
                                                  oob_is_err=False),
                             reads=[bxr[j % 2], bG], writes=[B_("dXS")])
                else:
                    p.dma("g", d_XSRT[j * 128:(j + 1) * 128, :], xr,
                          reads=[bxr[j % 2], bG], writes=[B_("dXS")])

        PS = [(pA, bpA), (pB, bpB), (pC, bpC), (pX[0], bpX[0]), (pX[1], bpX[1]),
              (pX[2], bpX[2]), (pX[3], bpX[3]), (pT, bpT)]

        def unit_D(u):
            if u == 0:
                gather_w(0, 1)
                gather_w(0, 2)
            gather_b(u)
            for r in range(4):
                xr = xrs[r % 2]
                p.dma("q", xr, d_XSRT[(u * 4 + r) * 128:(u * 4 + r + 1) * 128, :],
                      writes=[bxr[r % 2]], reads=[B_("dXS")])
                for cb in range(8):
                    tr(pT[:, 0:128], xr[:, cb * 128:(cb + 1) * 128], c_eye[:],
                       bpT, [bxr[r % 2]])
                    act(xT[:, cb * 512 + r * 128: cb * 512 + (r + 1) * 128],
                        pT[:, 0:128], AF.Copy, bxT, [bpT])
            def get(ct):
                return xT[:, ct * 512:(ct + 1) * 512], bxT
            ln_norm(get, False, BF2a, bk1=(pC, bpC), bk2=(pT, bpT))
            for ht in range(16):
                hp, bhp = (pA, bpA) if ht % 2 == 0 else (pB, bpB)
                for ct in range(8):
                    mm(hp[:, 0:512],
                       w1buf[:, ct * 2048 + ht * 128: ct * 2048 + (ht + 1) * 128],
                       BF2a[:, ct * 512:(ct + 1) * 512],
                       ct == 0, ct == 7, bhp, [bw1s, blnbf])
                act(ght[:, ht * 512:(ht + 1) * 512], hp[:, 0:512], AF.Gelu,
                    bgh, [bhp, B_("b1u")], bias=c_b1u[:, ht:ht + 1])
            if u + 1 < UCAP:
                gather_w(u + 1, 1)
            for h2 in range(2):
                for ht in range(16):
                    for c2 in range(4):
                        ct2 = h2 * 4 + c2
                        mm(pX[c2][:, 0:512],
                           w2buf[:, ht * 1024 + ct2 * 128: ht * 1024 + (ct2 + 1) * 128],
                           ght[:, ht * 512:(ht + 1) * 512],
                           ht == 0, ht == 15, bpX[c2], [bw2s, bgh])
                for c2 in range(4):
                    ct2 = h2 * 4 + c2
                    vts(o_f32[:, ct2 * 512:(ct2 + 1) * 512], pX[c2][:, 0:512],
                        c_b2u[:, ct2:ct2 + 1], None, ALU.add, ALU.bypass,
                        boacc, [bpX[c2], B_("b2u")])
            if u + 1 < UCAP:
                gather_w(u + 1, 2)
            for r in range(4):
                xr = xrs[r % 2]
                for cb in range(8):
                    tr(pT[:, 0:128], o_f32[:, cb * 512 + r * 128: cb * 512 + (r + 1) * 128],
                       c_eye[:], bpT, [boacc])
                    act(xr[:, cb * 128:(cb + 1) * 128], pT[:, 0:128], AF.Copy,
                        bxr[r % 2], [bpT])
                p.dma("g", d_OSRT[(u * 4 + r) * 128:(u * 4 + r + 1) * 128, :], xr,
                      reads=[bxr[r % 2]], writes=[B_("dOS")])

        def pass_E(phase):
            for sup in range(NSUP):
                for tk in range(4):
                    j = sup * 4 + tk
                    xr = xrs[j % 2]
                    xo = xos[j % 2]
                    if T_UNS:
                        p.dma_fn("g", lambda g, o=xr, ix=g_desti[:, j:j + 1]:
                                 g.indirect_dma_start(out=o, out_offset=None,
                                                      in_=d_OSRT,
                                                      in_offset=IOOA(ix, 0),
                                                      element_offset=0,
                                                      bounds_check=None,
                                                      oob_is_err=False),
                                 reads=[bG, B_("dOS")], writes=[bxr[j % 2]])
                    else:
                        p.dma("g", xr, d_OSRT[j * 128:(j + 1) * 128, :],
                              reads=[bG, B_("dOS")], writes=[bxr[j % 2]])
                    p.dma("q", xo,
                          (d_x if phase == 1 else d_X3)[j * 128:(j + 1) * 128, :],
                          writes=[bxo[j % 2]],
                          reads=[B_("dX3")] if phase == 4 else [])
                    vstt(xr, xr, g_scale[:, j:j + 1], xo, ALU.mult, ALU.add,
                         bxr[j % 2], [bxr[j % 2], bxo[j % 2], bG])
                    if phase == 1:
                        for cb in range(8):
                            tr(pT[:, 0:128], xr[:, cb * 128:(cb + 1) * 128], c_eye[:],
                               bpT, [bxr[j % 2]])
                            act(o_f32[:, cb * 512 + tk * 128: cb * 512 + (tk + 1) * 128],
                                pT[:, 0:128], AF.Copy, boacc, [bpT])
                    else:
                        p.dma("g", d_out[j * 128:(j + 1) * 128, :], xr,
                              reads=[bxr[j % 2]])
                if phase == 1:
                    for cb in range(8):
                        p.dma("g", d_R1[cb * 128:(cb + 1) * 128, sup * ST:(sup + 1) * ST],
                              o_f32[:, cb * 512:(cb + 1) * 512], reads=[boacc],
                              writes=[B_("dR1")])

        def ffn_sparse(phase):
            src_fn = p1_src if phase == 1 else p4A_src
            memset("v", g_s8[:, 0:1], 0.0, bG)
            for sup in range(NSUP):
                get_x = src_fn(sup)
                ln_norm(get_x, True, BF2a)
                routing_A(sup)
            p.barrier()
            unit_plan()
            if DEBUG and phase == 1:
                for i, t in enumerate((g_eid, g_rank, g_ssb, g_dest, g_scale)):
                    p.dma("g", d_DBG[i][:, 0:64], t[:], reads=[bG])
                p.dma("g", d_DBG[5][:, 0:24], g_expu[:], reads=[bG])
                p.dma("g", d_DBG[6][0:8, 0:128], g_s8[:], reads=[bG])
                p.dma("g", d_DBG[7][:, 0:24], g_w1i[:].bitcast(F32), reads=[bG])
            p.barrier()
            pass_C(d_x if phase == 1 else d_X3)
            p.barrier()
            for u in range(UCAP):
                unit_D(u)
            p.barrier()
            pass_E(phase)
            p.barrier()

        ffn_sparse(1)

        bscr = B_("scr")
        br1 = B_("r1")
        for ct in range(8):
            scr = AF8[:, 0:4096]
            p.dma("q", scr, d_winT[:, ct * 4096:(ct + 1) * 4096], writes=[bscr])
            vcopy(BF8[:, ct * 4096:(ct + 1) * 4096], scr, bw1, [bscr])
        for sup in range(NSUP):
            b = (sup * ST) // L
            t0 = (sup * ST) % L
            r1 = AF8[:, 4096:8192]
            for ct in range(8):
                p.dma("q", r1[:, ct * 512:(ct + 1) * 512],
                      d_R1[ct * 128:(ct + 1) * 128, sup * ST:(sup + 1) * ST],
                      writes=[br1])
            def get2(ct, r1=r1):
                return r1[:, ct * 512:(ct + 1) * 512], br1
            ln_norm(get2, False, BF2a)
            for mt in range(32):
                tgt = d_XS if mt < 16 else d_ZS
                dt_ = mt % 16
                for ct in range(8):
                    mm(pA[:, 0:512],
                       BF8[:, ct * 4096 + mt * 128: ct * 4096 + (mt + 1) * 128],
                       BF2a[:, ct * 512:(ct + 1) * 512], ct == 0, ct == 7,
                       bpA, [bw1, blnbf])
                ev = SM[:, 0:512] if mt % 2 == 0 else SM[:, 512:1024]
                act(ev, pA[:, 0:512], AF.Copy, bSM, [bpA])
                p.dma("g", tgt[b, dt_ * 128:(dt_ + 1) * 128, t0:t0 + 512], ev,
                      reads=[bSM])
        p.barrier()

        for half in range(2):
            scr = AF8[:, 0:8192]
            p.dma("q", scr, d_woutT[:, half * 8192:(half + 1) * 8192], writes=[bscr])
            vcopy(BF8[:, half * 8192:(half + 1) * 8192], scr, bw1, [bscr])
        woutT = BF8[:, 0:16384]
        xcbf = BF8[:, 16384:24576]
        yszb = BF8[:, 24576:32768]
        bxcbf = B_("xcbf")
        bysz = B_("ysz")

        XU = AF8[:, 0:8256]
        bXU = B_("XU")
        SLOT = [AF8[:, 8256 + i * 512: 8256 + (i + 1) * 512] for i in range(12)]
        bSL = [B_(f"sl{i}") for i in range(12)]
        Cb = BF2a
        bCb = B_("CbP3")
        sbs = [BF2b.bitcast(F32)[:, 0:512], BF2b.bitcast(F32)[:, 512:1024]]
        bsb = [B_("sb0"), B_("sb1")]
        h_ng = SM[:, 0:2048]
        bh = B_("h_ng")
        zero512 = SM[:, 2048:2560]
        bzero = B_("zero")
        dtc_bf = BFS[0:64, 0:512]
        bdtc = B_("dtc")
        Cv_bf = BFS[0:16, 512:1024]
        bCv = B_("Cv")
        Bv_bf = BFS[0:16, 1024:1536]
        bBv = B_("Bv")
        hC_big = BFS[:, 1536:3584]
        bhCb = B_("hCb")
        uDb = BFS[:, 3584:4096]
        buD = B_("uD")
        fin8 = U8a[:, 0:512]
        m28 = U8a[:, 512:1024]
        bu8f = B_("u8fin")
        bu8m = B_("u8m2")

        memset("v", zero512, 0.0, bzero)

        for b in range(B_LOC):
            for rev in (0, 1):
                dirp = rev
                memset("v", c_carry[:], 0.0, bcarry)
                chunks = range(NCH) if not rev else range(NCH - 1, -1, -1)
                for ci in chunks:
                    c0 = ci * CH
                    for dt_ in range(16):
                        xv = XU[:, dt_ * 516: dt_ * 516 + 516]
                        if not rev:
                            if c0 == 0:
                                memset("v", xv[:, 0:3], 0.0, bXU)
                                p.dma("q", xv[:, 3:515],
                                      d_XS[b, dt_ * 128:(dt_ + 1) * 128, 0:512],
                                      writes=[bXU])
                            else:
                                p.dma("q", xv[:, 0:515],
                                      d_XS[b, dt_ * 128:(dt_ + 1) * 128, c0 - 3:c0 + 512],
                                      writes=[bXU])
                        else:
                            if c0 + CH + 3 > L:
                                memset("v", xv[:, 512:515], 0.0, bXU)
                                p.dma("q", xv[:, 0:512],
                                      d_XS[b, dt_ * 128:(dt_ + 1) * 128, c0:c0 + 512],
                                      writes=[bXU])
                            else:
                                p.dma("q", xv[:, 0:515],
                                      d_XS[b, dt_ * 128:(dt_ + 1) * 128, c0:c0 + 515],
                                      writes=[bXU])
                    for dt_ in range(16):
                        xv = XU[:, dt_ * 516: dt_ * 516 + 516]
                        xc = SLOT[0]
                        for j in range(4):
                            wj = (j if not rev else 3 - j)
                            wcol = c_convw[:, dirp * 64 + dt_ * 4 + wj:
                                           dirp * 64 + dt_ * 4 + wj + 1]
                            if j == 0:
                                vts(xc, xv[:, 0:512], wcol, None, ALU.mult, ALU.bypass,
                                    bSL[0], [bXU, bconst])
                            else:
                                vstt(xc, xv[:, j:j + 512], wcol, xc, ALU.mult, ALU.add,
                                     bSL[0], [bXU, bconst, bSL[0]])
                        cbb = c_convb[:, dirp * 16 + dt_: dirp * 16 + dt_ + 1]
                        vts(xcbf[:, dt_ * 512:(dt_ + 1) * 512], xc, cbb, None,
                            ALU.add, ALU.bypass, bxcbf, [bSL[0], bconst])
                        vts(SLOT[1], xc, cbb, 10.0, ALU.add, ALU.min, bSL[1],
                            [bSL[0], bconst])
                        vts(XU[:, dt_ * 516: dt_ * 516 + 512], SLOT[1], -10.0, None,
                            ALU.max, ALU.bypass, bXU, [bSL[1]])
                    xo = dirp * 1536
                    for (m0, m1, ps_, bps) in ((0, 64, pA, bpA), (64, 80, pB, bpB),
                                               (80, 96, pC, bpC)):
                        for dt_ in range(16):
                            mm(ps_[0:m1 - m0, 0:512],
                               c_xprojT[:, xo + dt_ * 96 + m0: xo + dt_ * 96 + m1],
                               xcbf[:, dt_ * 512:(dt_ + 1) * 512],
                               dt_ == 0, dt_ == 15, bps, [bconst, bxcbf])
                    act(dtc_bf, pA[0:64, 0:512], AF.Copy, bdtc, [bpA])
                    act(Bv_bf, pB[0:16, 0:512], AF.Copy, bBv, [bpB])
                    act(Cv_bf, pC[0:16, 0:512], AF.Copy, bCv, [bpC])
                    for n in range(16):
                        mm(pT[:, 0:512], c_sel[:, n * 128:(n + 1) * 128], Cv_bf,
                           True, True, bpT, [bconst, bCv])
                        act(Cb[:, n * 512:(n + 1) * 512], pT[:, 0:512], AF.Copy,
                            bCb, [bpT])
                    for dt_ in range(16):
                        u_dt = XU[:, dt_ * 516: dt_ * 516 + 512]
                        mm(pA[:, 0:512],
                           c_dtprojT[:, dirp * 2048 + dt_ * 128: dirp * 2048 + (dt_ + 1) * 128],
                           dtc_bf, True, True, bpA, [bconst, bdtc])
                        dtb_ = c_dtb[:, dirp * 16 + dt_: dirp * 16 + dt_ + 1]
                        vts(SLOT[1], pA[:, 0:512], dtb_, None, ALU.add, ALU.bypass,
                            bSL[1], [bpA, bconst])
                        vts(SLOT[1], SLOT[1], 1e-5, 1.0, ALU.max, ALU.min, bSL[1], [bSL[1]])
                        act(SLOT[1], SLOT[1], AF.Exp, bSL[1], [bSL[1]], bias=dtb_)
                        delta = SLOT[2]
                        act(delta, SLOT[1], AF.Ln, bSL[2], [bSL[1]], bias=1.0)
                        du = SLOT[3]
                        vtt(du, delta, u_dt, ALU.mult, bSL[3], [bSL[2], bXU])
                        for ng in range(4):
                            for nl in range(4):
                                n = ng * 4 + nl
                                mm(pX[nl][:, 0:512], c_sel[:, n * 128:(n + 1) * 128],
                                   Bv_bf, True, True, bpX[nl], [bconst, bBv])
                                dA = SLOT[4] if nl % 2 == 0 else SLOT[5]
                                bdA = bSL[4] if nl % 2 == 0 else bSL[5]
                                act(dA, delta, AF.Exp, bdA, [bSL[2]],
                                    scale=c_A[:, dirp * 256 + dt_ * 16 + n:
                                              dirp * 256 + dt_ * 16 + n + 1])
                                sb = sbs[nl % 2]
                                vtt(sb, du, pX[nl][:, 0:512], ALU.mult, bsb[nl % 2],
                                    [bSL[3], bpX[nl]])
                                hsl = h_ng[:, nl * 512:(nl + 1) * 512]
                                init = c_carry[:, dt_ * 16 + n: dt_ * 16 + n + 1]
                                if not rev:
                                    vscan(hsl, dA, sb, init, bh,
                                          [bdA, bsb[nl % 2], bcarry])
                                else:
                                    vscan(hsl[:, ::-1], dA[:, ::-1], sb[:, ::-1],
                                          init, bh, [bdA, bsb[nl % 2], bcarry])
                            vtt(hC_big[:, 0:2048], h_ng[:, 0:2048],
                                Cb[:, ng * 2048:(ng + 1) * 2048], ALU.mult, bhCb,
                                [bh, bCb])
                            for nl in range(4):
                                mm(pB[:, 0:512], c_eyebf[:],
                                   hC_big[:, nl * 512:(nl + 1) * 512],
                                   (ng == 0 and nl == 0), False, bpB, [bconst, bhCb])
                            lastc = 511 if not rev else 0
                            vcopy(c_carry[:, dt_ * 16 + ng * 4: dt_ * 16 + ng * 4 + 4],
                                  h_ng[:, lastc::512], bcarry, [bh])
                        vts(uDb, u_dt, c_Dv[:, dirp * 16 + dt_: dirp * 16 + dt_ + 1],
                            None, ALU.mult, ALU.bypass, buD, [bXU, bconst])
                        mm(pB[:, 0:512], c_eyebf[:], uDb, False, True, bpB,
                           [bconst, buD])
                        y1 = SLOT[9]
                        act(y1, pB[:, 0:512], AF.Copy, bSL[9], [bpB])
                        if not rev:
                            p.dma("g", d_YF[b, dt_ * 128:(dt_ + 1) * 128, c0:c0 + 512],
                                  y1, reads=[bSL[9]])
                        else:
                            yf = SLOT[6]
                            zt = SLOT[7]
                            p.dma("q", yf, d_YF[b, dt_ * 128:(dt_ + 1) * 128, c0:c0 + 512],
                                  writes=[bSL[6]])
                            p.dma("q", zt, d_ZS[b, dt_ * 128:(dt_ + 1) * 128, c0:c0 + 512],
                                  writes=[bSL[7]])
                            vtt(y1, y1, yf, ALU.add, bSL[9], [bSL[9], bSL[6]])
                            sz = SLOT[8]
                            act(sz, zt, AF.Tanh, bSL[8], [bSL[7]], scale=c_k[:, 0:1])
                            vts(sz, sz, 0.5, 0.5, ALU.mult, ALU.add, bSL[8], [bSL[8]])
                            vtt(sz, sz, zt, ALU.mult, bSL[8], [bSL[8], bSL[7]])
                            vtt(yszb[:, dt_ * 512:(dt_ + 1) * 512], y1, sz, ALU.mult,
                                bysz, [bSL[9], bSL[8]])
                    if rev:
                        for ct in range(8):
                            op_, bop = (pA, bpA) if ct % 2 == 0 else (pC, bpC)
                            for dt_ in range(16):
                                mm(op_[:, 0:512],
                                   woutT[:, dt_ * 1024 + ct * 128: dt_ * 1024 + (ct + 1) * 128],
                                   yszb[:, dt_ * 512:(dt_ + 1) * 512],
                                   dt_ == 0, dt_ == 15, bop, [bw1, bysz])
                            o1 = SLOT[10]
                            act(o1, op_[:, 0:512], AF.Copy, bSL[10], [bop])
                            act(fin8, op_[:, 0:512], AF.Is_finite, bu8f, [bop])
                            sg = SLOT[8]
                            act(sg, op_[:, 0:512], AF.Sign, bSL[8], [bop])
                            s1 = SLOT[1]
                            p.op("v", lambda v, o=s1, m=fin8, a=o1, bb=sg:
                                 v.select(o, m, a, bb),
                                 reads=[bu8f, bSL[10], bSL[8]], writes=[bSL[1]])
                            vtt(m28, s1, s1, ALU.is_equal, bu8m, [bSL[1]])
                            s2 = SLOT[10]
                            p.op("v", lambda v, o=s2, m=m28, a=s1, bb=zero512:
                                 v.select(o, m, a, bb),
                                 reads=[bu8m, bSL[1], bzero], writes=[bSL[10]])
                            r1t = SLOT[11]
                            p.dma("q", r1t, d_R1[ct * 128:(ct + 1) * 128,
                                                 b * L + c0: b * L + c0 + 512],
                                  writes=[bSL[11]])
                            x2 = SLOT[1]
                            vtt(x2, s2, r1t, ALU.add, bSL[1], [bSL[10], bSL[11]])
                            p.dma("g", d_R2[ct * 128:(ct + 1) * 128,
                                            b * L + c0: b * L + c0 + 512], x2,
                                  reads=[bSL[1]])
                p.barrier()
        p.barrier()

        ffn_sparse(4)

        p.cmds["g"].append([lambda eng: eng.wait_ge(sems["g"], p.cnt["g"]), None])
        p.cmds["q"].append([lambda eng: eng.wait_ge(sems["q"], p.cnt["q"]), None])

        block = stk.enter_context(nc.Block())

        def replay(eng_obj, ename):
            for fn, inc in p.cmds[ename]:
                ins = fn(eng_obj)
                if inc:
                    ins.then_inc(sems[ename], inc)

        @block.tensor
        def _(t):
            replay(t, "t")

        @block.vector
        def _(v):
            replay(v, "v")

        @block.scalar
        def _(s):
            replay(s, "s")

        @block.sync
        def _(q):
            replay(q, "q")

        @block.gpsimd
        def _(g):
            replay(g, "g")

    return nc


def _prep_inputs(inp):
    g = {k: np.asarray(v) for k, v in inp.items()}
    f = np.float32
    out = {}
    out["winT"] = np.ascontiguousarray(
        g["L0_attn_in_proj_w"].T.reshape(8, 128, 4096).transpose(1, 0, 2)
        .reshape(128, 8 * 4096)).astype(f, copy=False)
    out["woutT"] = np.ascontiguousarray(
        g["L0_attn_out_proj_w"].T.reshape(16, 128, 1024).transpose(1, 0, 2)
        .reshape(128, 16 * 1024)).astype(f, copy=False)
    xp = np.stack([np.asarray(g["L0_attn_x_proj_w"]).T,
                   np.asarray(g["L0_attn_x_proj_b_w"]).T])
    out["xprojT"] = np.ascontiguousarray(
        xp.reshape(2, 16, 128, 96).transpose(2, 0, 1, 3).reshape(128, 2 * 1536)
    ).astype(f, copy=False)
    dtp = np.stack([np.asarray(g["L0_attn_dt_proj_w"]).T,
                    np.asarray(g["L0_attn_dt_proj_b_w"]).T])
    out["dtprojT"] = np.ascontiguousarray(
        dtp.transpose(1, 0, 2).reshape(64, 2 * 2048)).astype(f, copy=False)
    cw = np.stack([g["L0_attn_conv1d_w"].reshape(2048, 4),
                   g["L0_attn_conv1d_b_w"].reshape(2048, 4)])
    out["convw"] = np.ascontiguousarray(
        cw.reshape(2, 16, 128, 4).transpose(2, 0, 1, 3).reshape(128, 2 * 64)
    ).astype(f, copy=False)
    def col2(a, b):
        s = np.stack([np.asarray(a).reshape(16, 128), np.asarray(b).reshape(16, 128)])
        return np.ascontiguousarray(s.transpose(2, 0, 1).reshape(128, 32)).astype(f, copy=False)
    out["convb"] = col2(g["L0_attn_conv1d_bias"], g["L0_attn_conv1d_b_bias"])
    out["dtb"] = col2(g["L0_attn_dt_proj_bias"], g["L0_attn_dt_proj_b_bias"])
    out["Dv"] = col2(g["L0_attn_D"], g["L0_attn_D_b"])
    al = np.stack([g["L0_attn_A_log"], g["L0_attn_A_b_log"]])
    out["Alog"] = np.ascontiguousarray(
        al.reshape(2, 16, 128, 16).transpose(2, 0, 1, 3).reshape(128, 512)
    ).astype(f, copy=False)
    out["wred"] = np.ascontiguousarray(
        g["L0_ffn_w_red"].reshape(8, 128, 16).transpose(1, 0, 2).reshape(128, 128)
    ).astype(f, copy=False)
    out["wg"] = np.ascontiguousarray(g["L0_ffn_wg"]).astype(f, copy=False)
    out["W1"] = np.ascontiguousarray(g["L0_ffn_W1"]).astype(f, copy=False)
    out["W2"] = np.ascontiguousarray(g["L0_ffn_W2"]).astype(f, copy=False)
    out["b1c"] = np.ascontiguousarray(
        g["L0_ffn_b1"].reshape(8, 16, 128).transpose(2, 0, 1).reshape(128, 128)
    ).astype(f, copy=False)
    out["b2c"] = np.ascontiguousarray(
        g["L0_ffn_b2"].reshape(8, 8, 128).transpose(2, 0, 1).reshape(128, 64)
    ).astype(f, copy=False)
    out["iotap"] = np.stack([np.arange(128), 8 * np.arange(128)], axis=1).astype(f)
    out["evrow8"] = np.tile(np.arange(8).astype(f), (128, 1))
    out["u24"] = np.tile(np.arange(24).astype(f), (8, 1))
    out["eye_f32"] = np.eye(128, dtype=f)
    out["eye_bf"] = np.eye(128).astype(BFNP)
    out["ones_f32"] = np.ones((128, 128), f)
    out["ones_bf"] = np.ones((16, 512), BFNP)
    sel = np.zeros((16, 16 * 128), np.float32)
    for n in range(16):
        sel[n, n * 128:(n + 1) * 128] = 1.0
    out["sel_bf"] = sel.astype(BFNP)
    return out


def kernel(**inp):
    shared = _prep_inputs(inp)
    x = np.ascontiguousarray(
        np.asarray(inp["L0_attn_hidden_states"], dtype=np.float32).reshape(65536, 1024))
    nc = build_nc()
    in_maps = []
    for c in range(NCORE):
        m = dict(shared)
        m["x"] = np.ascontiguousarray(x[c * TOK:(c + 1) * TOK])
        in_maps.append(m)
    res = run_bass_kernel_spmd(nc, in_maps, core_ids=list(range(NCORE)))
    out = np.concatenate([np.asarray(r["out"]) for r in res.results], axis=0)
    return out.reshape(16, 4096, 1024)


if __name__ == "__main__":
    build_nc()
    print("built ok")


# version 15
# speedup vs baseline: 1.2466x; 1.1174x over previous
import numpy as np
import ml_dtypes
import concourse.bass as bass
import concourse.mybir as mybir
from concourse.bass_utils import run_bass_kernel_spmd
from contextlib import ExitStack

F32 = mybir.dt.float32
BF16 = mybir.dt.bfloat16
U8 = mybir.dt.uint8
AF = mybir.ActivationFunctionType
ALU = mybir.AluOpType
AX = mybir.AxisListType
BFNP = ml_dtypes.bfloat16

NCORE = 8
B_LOC = 2
L = 4096
CDIM = 1024
DIN = 2048
TOK = B_LOC * L
ST = 512
NSUP = TOK // ST
CH = 512
NCH = L // CH
UCAP = 23
NTT = TOK // 128

DEBUG = False
T_SCAT = True
T_WG = True
T_BG = True
T_UNS = True

ENGINES = ("t", "v", "s", "q", "g")


class Buf:
    __slots__ = ("w", "readers", "name")
    def __init__(self, name):
        self.name = name
        self.w = None
        self.readers = set()


class Prog:
    def __init__(self, nc, sems):
        self.nc = nc
        self.sems = sems
        self.cmds = {e: [] for e in ENGINES}
        self.cnt = {e: 0 for e in ENGINES}
        self.lastidx = {e: -1 for e in ENGINES}
        self.markidx = {e: -1 for e in ENGINES}
        self.waited = {}

    def _emit(self, e, fn):
        self.cmds[e].append([fn, 0])
        self.lastidx[e] = len(self.cmds[e]) - 1
        return self.lastidx[e]

    def mark(self, e):
        li = self.lastidx[e]
        if li < 0:
            return (e, 0)
        if self.markidx[e] == li:
            return (e, self.cnt[e])
        inc = 16 if e in ("q", "g") else 1
        self.cmds[e][li][1] += inc
        self.cnt[e] += inc
        self.markidx[e] = li
        return (e, self.cnt[e])

    def wait(self, consumer, evt):
        pe, val = evt
        if val <= 0 or pe == consumer:
            return
        key = (consumer, pe)
        if self.waited.get(key, 0) >= val:
            return
        self.waited[key] = val
        sem = self.sems[pe]
        self.cmds[consumer].append(
            [lambda eng, s=sem, v=val: eng.wait_ge(s, v), None])

    def _hazards(self, e, reads, writes):
        for b in reads:
            if b.w is not None and b.w != e:
                self.wait(e, self.mark(b.w))
        for b in writes:
            if b.w is not None and b.w != e:
                self.wait(e, self.mark(b.w))
            for r in list(b.readers):
                if r != e:
                    self.wait(e, self.mark(r))
        for b in reads:
            b.readers.add(e)
        for b in writes:
            b.w = e
            b.readers = set()

    def op(self, e, fn, reads=(), writes=()):
        self._hazards(e, reads, writes)
        return self._emit(e, fn)

    def dma(self, e, out, in_, reads=(), writes=()):
        assert e in ("q", "g")
        self._hazards(e, reads, writes)
        idx = self._emit(e, lambda eng, o=out, i=in_: eng.dma_start(out=o, in_=i))
        self.cmds[e][idx][1] = 16
        self.cnt[e] += 16
        self.markidx[e] = idx

    def dma_fn(self, e, fn, reads=(), writes=()):
        self._hazards(e, reads, writes)
        idx = self._emit(e, fn)
        self.cmds[e][idx][1] = 16
        self.cnt[e] += 16
        self.markidx[e] = idx

    def barrier(self):
        evts = [self.mark(e) for e in ENGINES]
        for c in ENGINES:
            for evt in evts:
                self.wait(c, evt)


def build_nc():
    nc = bass.Bass("TRN2", target_bir_lowering=False)
    scratch_kind = "ExternalOutput" if DEBUG else "Internal"

    d_x = nc.dram_tensor("x", [TOK, CDIM], F32, kind="ExternalInput").ap()
    d_out = nc.dram_tensor("out", [TOK, CDIM], F32, kind="ExternalOutput").ap()

    d_winT = nc.dram_tensor("winT", [128, 8 * 4096], F32, kind="ExternalInput").ap()
    d_woutT = nc.dram_tensor("woutT", [128, 16 * 1024], F32, kind="ExternalInput").ap()
    d_xprojT = nc.dram_tensor("xprojT", [128, 2 * 1536], F32, kind="ExternalInput").ap()
    d_dtprojT = nc.dram_tensor("dtprojT", [64, 2 * 2048], F32, kind="ExternalInput").ap()
    d_convw = nc.dram_tensor("convw", [128, 2 * 64], F32, kind="ExternalInput").ap()
    d_convb = nc.dram_tensor("convb", [128, 2 * 16], F32, kind="ExternalInput").ap()
    d_dtb = nc.dram_tensor("dtb", [128, 2 * 16], F32, kind="ExternalInput").ap()
    d_Alog = nc.dram_tensor("Alog", [128, 2 * 256], F32, kind="ExternalInput").ap()
    d_Dv = nc.dram_tensor("Dv", [128, 2 * 16], F32, kind="ExternalInput").ap()
    d_wred = nc.dram_tensor("wred", [128, 128], F32, kind="ExternalInput").ap()
    d_wg = nc.dram_tensor("wg", [8, 16], F32, kind="ExternalInput").ap()
    d_W1 = nc.dram_tensor("W1", [8, 1024, 2048], F32, kind="ExternalInput").ap()
    d_b1c = nc.dram_tensor("b1c", [128, 128], F32, kind="ExternalInput").ap()
    d_W2 = nc.dram_tensor("W2", [8, 2048, 1024], F32, kind="ExternalInput").ap()
    d_b2c = nc.dram_tensor("b2c", [128, 64], F32, kind="ExternalInput").ap()
    d_iotap = nc.dram_tensor("iotap", [128, 2], F32, kind="ExternalInput").ap()
    d_evrow8 = nc.dram_tensor("evrow8", [128, 8], F32, kind="ExternalInput").ap()
    d_u24 = nc.dram_tensor("u24", [8, 24], F32, kind="ExternalInput").ap()
    d_eye = nc.dram_tensor("eye_f32", [128, 128], F32, kind="ExternalInput").ap()
    d_eyebf = nc.dram_tensor("eye_bf", [128, 128], BF16, kind="ExternalInput").ap()
    d_ones = nc.dram_tensor("ones_f32", [128, 128], F32, kind="ExternalInput").ap()
    d_onesbf = nc.dram_tensor("ones_bf", [16, 512], BF16, kind="ExternalInput").ap()
    d_sel = nc.dram_tensor("sel_bf", [16, 16 * 128], BF16, kind="ExternalInput").ap()

    if DEBUG:
        d_DBG = nc.dram_tensor("DBG", [16, 128, 512], F32, kind="ExternalOutput").ap()
        d_DBGU8 = nc.dram_tensor("DBGU8", [128, 8192], U8, kind="ExternalOutput").ap()
        d_DBGBF = nc.dram_tensor("DBGBF", [8, 128, 512], BF16, kind="ExternalOutput").ap()
    else:
        d_DBG = d_DBGU8 = d_DBGBF = None
    d_R1 = nc.dram_tensor("R1", [CDIM, TOK], F32, kind=scratch_kind).ap()
    d_R2 = nc.dram_tensor("R2", [CDIM, TOK], F32, kind=scratch_kind).ap()
    d_XS = nc.dram_tensor("XS", [B_LOC, DIN, L], F32, kind=scratch_kind).ap()
    d_ZS = nc.dram_tensor("ZS", [B_LOC, DIN, L], F32, kind=scratch_kind).ap()
    d_YF = nc.dram_tensor("YF", [B_LOC, DIN, L], F32, kind=scratch_kind).ap()
    d_W1B = nc.dram_tensor("W1B", [1024, 16384], BF16, kind="Internal").ap()
    d_W2B = nc.dram_tensor("W2B", [1024, 16384], BF16, kind="Internal").ap()
    d_XSRT = nc.dram_tensor("XSRT", [UCAP * ST, CDIM], F32, kind=scratch_kind).ap()
    d_OSRT = nc.dram_tensor("OSRT", [UCAP * ST, CDIM], F32, kind=scratch_kind).ap()
    d_X3 = nc.dram_tensor("X3", [TOK, CDIM], F32, kind=scratch_kind).ap()

    with ExitStack() as stk:
        def sbuf(name, shape, dt):
            return stk.enter_context(nc.sbuf_tensor(name, shape, dt))
        def psum(name, shape, dt):
            return stk.enter_context(nc.psum_tensor(name, shape, dt))

        AF8 = sbuf("AF8", [128, 14400], F32)
        AF8bf = AF8.bitcast(BF16)
        BF8 = sbuf("BF8", [128, 34816], BF16)
        BF2a = sbuf("BF2a", [128, 8192], BF16)
        BF2b = sbuf("BF2b", [128, 2048], BF16)
        U8a = sbuf("U8a", [128, 1024], U8)
        SM = sbuf("SM", [128, 2560], F32)
        BFS = sbuf("BFS", [128, 4096], BF16)
        I32 = mybir.dt.int32
        g_eid = sbuf("g_eid", [128, 64], F32)
        g_rank = sbuf("g_rank", [128, 64], F32)
        g_ssb = sbuf("g_ssb", [128, 64], F32)
        g_dest = sbuf("g_dest", [128, 64], F32)
        g_desti = sbuf("g_desti", [128, 64], I32)
        g_scale = sbuf("g_scale", [128, 64], F32)
        g_expu = sbuf("g_expu", [128, 24], F32)
        g_w1i = sbuf("g_w1i", [128, 24], I32)
        g_w2i = sbuf("g_w2i", [128, 24], I32)
        g_bii = sbuf("g_bii", [128, 24], I32)
        g_vb = sbuf("g_vb", [128, 8], F32)
        g_s8 = sbuf("g_s8", [8, 128], F32)
        c_one8 = sbuf("c_one8", [8, 512], F32)
        c_iotap = sbuf("c_iotap", [128, 2], F32)
        c_evrow8 = sbuf("c_evrow8", [128, 8], F32)
        c_u24 = sbuf("c_u24", [8, 24], F32)
        c_b1u = sbuf("c_b1u", [128, 16], F32)
        c_b2u = sbuf("c_b2u", [128, 8], F32)
        c_b1all = sbuf("c_b1all", [128, 128], F32)
        c_b2all = sbuf("c_b2all", [128, 64], F32)
        c_ind = sbuf("c_ind", [128, 8], F32)
        c_sp = sbuf("c_sp", [128, 32], F32)
        c_ws = sbuf("c_ws", [1, 16], F32)
        c_z = sbuf("c_z", [128, 1], F32)
        c_eye = sbuf("c_eye", [128, 128], F32)
        c_eyebf = sbuf("c_eyebf", [128, 128], BF16)
        c_ones = sbuf("c_ones", [128, 128], F32)
        c_onesbf = sbuf("c_onesbf", [16, 512], BF16)
        c_sel = sbuf("c_sel", [16, 16 * 128], BF16)
        c_wred = sbuf("c_wred", [128, 128], F32)
        c_convw = sbuf("c_convw", [128, 128], F32)
        c_convb = sbuf("c_convb", [128, 32], F32)
        c_dtb = sbuf("c_dtb", [128, 32], F32)
        c_Dv = sbuf("c_Dv", [128, 32], F32)
        c_A = sbuf("c_A", [128, 512], F32)
        c_xprojT = sbuf("c_xprojT", [128, 2 * 1536], BF16)
        c_dtprojT = sbuf("c_dtprojT", [64, 2 * 2048], BF16)
        c_wgn = sbuf("c_wgn", [8, 16], F32)
        c_wgnT = sbuf("c_wgnT", [16, 8], F32)
        c_carry = sbuf("c_carry", [128, 256], F32)
        c_k = sbuf("c_k", [128, 4], F32)

        pA = psum("pA", [128, 512], F32)
        pB = psum("pB", [128, 512], F32)
        pC = psum("pC", [128, 512], F32)
        pX = [psum(f"pX{i}", [128, 512], F32) for i in range(4)]
        pT = psum("pT", [128, 512], F32)

        sems = {e: stk.enter_context(nc.semaphore(f"sem_{e}")) for e in ENGINES}
        p = Prog(nc, sems)

        bufs = {}
        def B_(name):
            if name not in bufs:
                bufs[name] = Buf(name)
            return bufs[name]
        bconst = B_("const")
        bSM = B_("SM")
        bpA, bpB, bpC, bpT = B_("pA"), B_("pB"), B_("pC"), B_("pT")
        bpX = [B_(f"pX{i}") for i in range(4)]
        bcarry = B_("carry")

        def mm(ps_ap, lhsT, rhs, start, stop, wps, rds):
            p.op("t", lambda t, o=ps_ap, l=lhsT, r=rhs, a=start, b=stop:
                 t.matmul(o, l, r, start=a, stop=b), reads=rds, writes=[wps])

        def tr(ps_ap, in_ap, ident, wps, rds):
            p.op("t", lambda t, o=ps_ap, i=in_ap, e=ident:
                 t.transpose(o, i, e), reads=rds, writes=[wps])

        def act(out_ap, in_ap, func, wbuf, rds, bias=0.0, scale=1.0):
            p.op("s", lambda s, o=out_ap, i=in_ap, f=func, b=bias, sc=scale:
                 s.activation(o, i, f, bias=b, scale=sc), reads=rds, writes=[wbuf])

        def vcopy(out_ap, in_ap, wbuf, rds):
            p.op("v", lambda v, o=out_ap, i=in_ap: v.tensor_copy(o, i),
                 reads=rds, writes=[wbuf])

        def vtt(out_ap, a_ap, b_ap, alu, wbuf, rds):
            p.op("v", lambda v, o=out_ap, a=a_ap, b=b_ap, u=alu:
                 v.tensor_tensor(o, a, b, u), reads=rds, writes=[wbuf])

        def vts(out_ap, in_ap, s1, s2, op0, op1, wbuf, rds):
            if s2 is None:
                p.op("v", lambda v, o=out_ap, i=in_ap, a=s1, u0=op0:
                     v.tensor_scalar(o, i, a, None, u0), reads=rds, writes=[wbuf])
            else:
                p.op("v", lambda v, o=out_ap, i=in_ap, a=s1, b=s2, u0=op0, u1=op1:
                     v.tensor_scalar(o, i, a, b, u0, u1), reads=rds, writes=[wbuf])

        def vstt(out_ap, in0, scal, in1, op0, op1, wbuf, rds):
            p.op("v", lambda v, o=out_ap, i0=in0, sc=scal, i1=in1, u0=op0, u1=op1:
                 v.scalar_tensor_tensor(o, i0, sc, i1, u0, u1), reads=rds, writes=[wbuf])

        def gts(out_ap, in_ap, s1, op0, wbuf, rds):
            p.op("g", lambda g, o=out_ap, i=in_ap, a=s1, u0=op0:
                 g.tensor_scalar(o, i, a, None, u0), reads=rds, writes=[wbuf])

        def gtt(out_ap, a_ap, b_ap, alu, wbuf, rds):
            p.op("g", lambda g, o=out_ap, a=a_ap, b=b_ap, u=alu:
                 g.tensor_tensor(o, a, b, u), reads=rds, writes=[wbuf])

        def vscan(out_ap, d0, d1, init, wbuf, rds):
            p.op("v", lambda v, o=out_ap, a=d0, b=d1, i=init:
                 v.tensor_tensor_scan(o, a, b, i, ALU.mult, ALU.add),
                 reads=rds, writes=[wbuf])

        def vred(out_ap, in_ap, alu_max, wbuf, rds):
            if alu_max:
                p.op("v", lambda v, o=out_ap, i=in_ap:
                     v.reduce_max(o, i, axis=AX.X), reads=rds, writes=[wbuf])
            else:
                p.op("v", lambda v, o=out_ap, i=in_ap:
                     v.reduce_sum(o, i, axis=AX.X), reads=rds, writes=[wbuf])

        def memset(e, ap, val, wbuf):
            p.op(e, lambda eng, a=ap, c=val: eng.memset(a, c), writes=[wbuf])

        for dst, src in [
            (c_eye[:], d_eye), (c_eyebf[:], d_eyebf), (c_ones[:], d_ones),
            (c_onesbf[:], d_onesbf), (c_sel[:], d_sel),
            (c_wred[:], d_wred), (c_convw[:], d_convw),
            (c_convb[:], d_convb), (c_dtb[:], d_dtb), (c_Dv[:], d_Dv),
            (c_iotap[:], d_iotap), (c_evrow8[:], d_evrow8), (c_u24[:], d_u24),
            (c_b1all[:], d_b1c), (c_b2all[:], d_b2c),
        ]:
            p.dma("q", dst, src, writes=[bconst])
        memset("v", c_k[:, 0:1], 0.5, bconst)
        memset("v", c_k[:, 1:2], -0.5, bconst)
        memset("v", c_k[:, 2:3], -1.0, bconst)
        memset("v", c_k[:, 3:4], -0.6931471805599453, bconst)
        memset("v", c_one8[:], 1.0, bconst)
        memset("v", c_z[:], 0.0, bconst)
        p.dma("q", SM[:, 0:512], d_Alog, writes=[bSM])
        act(c_A[:], SM[:, 0:512], AF.Exp, bconst, [bSM])
        vts(c_A[:], c_A[:], -1.0, None, ALU.mult, ALU.bypass, bconst, [bconst])
        p.dma("q", SM[:, 0:1536], d_xprojT[:, 0:1536], writes=[bSM])
        vcopy(c_xprojT[:, 0:1536], SM[:, 0:1536], bconst, [bSM])
        p.dma("q", SM[:, 0:1536], d_xprojT[:, 1536:3072], writes=[bSM])
        vcopy(c_xprojT[:, 1536:3072], SM[:, 0:1536], bconst, [bSM])
        p.dma("q", SM[0:64, 0:2048], d_dtprojT[:, 0:2048], writes=[bSM])
        vcopy(c_dtprojT[:, 0:2048], SM[0:64, 0:2048], bconst, [bSM])
        p.dma("q", SM[0:64, 0:2048], d_dtprojT[:, 2048:4096], writes=[bSM])
        vcopy(c_dtprojT[:, 2048:4096], SM[0:64, 0:2048], bconst, [bSM])
        p.dma("q", c_wgn[:], d_wg, writes=[bconst])
        act(SM[0:8, 0:16], c_wgn[:], AF.Square, bSM, [bconst])
        vred(SM[0:8, 16:17], SM[0:8, 0:16], False, bSM, [bSM])
        act(SM[0:8, 17:18], SM[0:8, 16:17], AF.Sqrt, bSM, [bSM])
        vts(SM[0:8, 18:19], SM[0:8, 17:18], 1e-4, None, ALU.max, ALU.bypass, bSM, [bSM])
        act(SM[0:8, 21:22], SM[0:8, 18:19], AF.Ln, bSM, [bSM])
        act(SM[0:8, 22:23], SM[0:8, 0:1], AF.Square, bSM, [bSM])
        act(SM[0:8, 23:24], SM[0:8, 0:1], AF.Square, bSM, [bSM])
        act(SM[0:8, 19:20], SM[0:8, 21:22], AF.Exp, bSM, [bSM], scale=c_k[0:8, 2:3])
        vts(c_wgn[:], c_wgn[:], SM[0:8, 19:20], None, ALU.mult, ALU.bypass,
            bconst, [bconst, bSM])
        mm(pT[0:16, 0:8], c_wgn[:], c_eye[0:8, 0:8], True, True, bpT, [bconst])
        act(c_wgnT[:], pT[0:16, 0:8], AF.Copy, bconst, [bpT])
        for ct in range(8):
            mm(pT[0:1, 16:32], c_ones[:, 0:1], c_wred[:, ct * 16:(ct + 1) * 16],
               ct == 0, ct == 7, bpT, [bconst])
        act(c_ws[:], pT[0:1, 16:32], AF.Copy, bconst, [bpT])

        bwc = [B_("wc0"), B_("wc1")]
        bwcb = [B_("wcb0"), B_("wcb1")]
        for src, dst, rows, w in ((d_W1, d_W1B, 1024, 2048), (d_W2, d_W2B, 2048, 1024)):
            for e in range(8):
                for tile in range(rows // 128):
                    i = tile % 2
                    sl = AF8[:, 4096 + i * w: 4096 + (i + 1) * w]
                    bfl = BFS[:, i * w:(i + 1) * w]
                    p.dma("q", sl, src[e, tile * 128:(tile + 1) * 128, :], writes=[bwc[i]])
                    vcopy(bfl, sl, bwcb[i], [bwc[i]])
                    p.dma("g", dst[e * 128:(e + 1) * 128, tile * w:(tile + 1) * w],
                          bfl, reads=[bwcb[i]])

        bxT = B_("xT")
        boacc = B_("oacc")
        bgh = B_("gh")
        blnbf = B_("lnbf")
        bmask = B_("mask")
        bw1 = B_("w1")
        bw2 = B_("w2")

        def ln_norm(get_x, routing, lnbf, bk1=None, bk2=None):
            pA_, bpA_ = bk1 if bk1 is not None else (pA, bpA)
            pB_, bpB_ = bk2 if bk2 is not None else (pB, bpB)
            for ct in range(8):
                xct, bx = get_x(ct)
                act(SM[:, 0:512], xct, AF.Square, bSM, [bx])
                mm(pA_[0:1, 0:512], c_ones[:, 0:1], xct, ct == 0, ct == 7, bpA_,
                   [bx, bconst])
                mm(pB_[0:1, 0:512], c_ones[:, 0:1], SM[:, 0:512], ct == 0, ct == 7,
                   bpB_, [bSM, bconst])
            A_ = SM[0:1, 1024:1536]
            Bs = SM[0:1, 1536:2048]
            Cs = SM[0:1, 2048:2560]
            act(A_, pA_[0:1, 0:512], AF.Copy, bSM, [bpA_])
            act(Bs, pB_[0:1, 0:512], AF.Copy, bSM, [bpB_])
            vts(A_, A_, 1.0 / CDIM, None, ALU.mult, ALU.bypass, bSM, [bSM])
            vts(Bs, Bs, 1.0 / CDIM, None, ALU.mult, ALU.bypass, bSM, [bSM])
            vtt(Cs, A_, A_, ALU.mult, bSM, [bSM])
            vtt(Bs, Bs, Cs, ALU.subtract, bSM, [bSM])
            vts(Bs, Bs, 1e-5, None, ALU.add, ALU.bypass, bSM, [bSM])
            act(Cs, Bs, AF.Ln, bSM, [bSM])
            act(Bs, Cs, AF.Exp, bSM, [bSM], scale=c_k[0:1, 1:2])
            vtt(Cs, A_, Bs, ALU.mult, bSM, [bSM])
            mm(pA_[:, 0:512], c_ones[0:1, :], Bs, True, True, bpA_, [bSM, bconst])
            mm(pB_[:, 0:512], c_ones[0:1, :], Cs, True, True, bpB_, [bSM, bconst])
            for ct in range(8):
                xct, bx = get_x(ct)
                t1 = SM[:, 0:512]
                vtt(t1, xct, pA_[:, 0:512], ALU.mult, bSM, [bx, bpA_])
                lnf = SM[:, 512:1024]
                vtt(lnf, t1, pB_[:, 0:512], ALU.subtract, bSM, [bSM, bpB_])
                vcopy(lnbf[:, ct * 512:(ct + 1) * 512], lnf, blnbf, [bSM])
                if routing:
                    mm(pC[0:16, 0:512], c_wred[:, ct * 16:(ct + 1) * 16], lnf,
                       ct == 0, ct == 7, bpC, [bSM, bconst])

        bG = B_("G")

        def lnstats_route(get_x):
            for ct in range(8):
                xct, bx = get_x(ct)
                act(SM[:, 0:512], xct, AF.Square, bSM, [bx])
                mm(pA[0:1, 0:512], c_ones[:, 0:1], xct, ct == 0, ct == 7, bpA,
                   [bx, bconst])
                mm(pB[0:1, 0:512], c_ones[:, 0:1], SM[:, 0:512], ct == 0, ct == 7,
                   bpB, [bSM, bconst])
                mm(pC[0:16, 0:512], c_wred[:, ct * 16:(ct + 1) * 16], xct,
                   ct == 0, ct == 7, bpC, [bx, bconst])
            A_ = SM[0:1, 1024:1536]
            Bs = SM[0:1, 1536:2048]
            Cs = SM[0:1, 2048:2560]
            act(A_, pA[0:1, 0:512], AF.Copy, bSM, [bpA])
            act(Bs, pB[0:1, 0:512], AF.Copy, bSM, [bpB])
            vts(A_, A_, 1.0 / CDIM, None, ALU.mult, ALU.bypass, bSM, [bSM])
            vts(Bs, Bs, 1.0 / CDIM, None, ALU.mult, ALU.bypass, bSM, [bSM])
            vtt(Cs, A_, A_, ALU.mult, bSM, [bSM])
            vtt(Bs, Bs, Cs, ALU.subtract, bSM, [bSM])
            vts(Bs, Bs, 1e-5, None, ALU.add, ALU.bypass, bSM, [bSM])
            act(Cs, Bs, AF.Ln, bSM, [bSM])
            act(Bs, Cs, AF.Exp, bSM, [bSM], scale=c_k[0:1, 1:2])
            mm(pT[0:16, 0:512], c_ws[:], A_, True, True, bpT, [bSM, bconst])
            mm(pA[0:16, 0:512], c_ones[0:1, 0:16], Bs, True, True, bpA,
               [bSM, bconst])
            xred = SM[0:16, 0:512]
            act(xred, pC[0:16, 0:512], AF.Copy, bSM, [bpC])
            vtt(xred, xred, pT[0:16, 0:512], ALU.subtract, bSM, [bSM, bpT])
            vtt(xred, xred, pA[0:16, 0:512], ALU.mult, bSM, [bSM, bpA])

        def routing_A(sup):
            xred = SM[0:16, 0:512]
            mm(pC[0:8, 0:512], c_wgnT[:], xred, True, True, bpC, [bSM, bconst])
            lg = SM[0:8, 512:1024]
            act(lg, pC[0:8, 0:512], AF.Copy, bSM, [bpC])
            ltok = SM[:, 0:32]
            for tb in range(4):
                tr(pT[0:128, 0:8], lg[:, tb * 128:(tb + 1) * 128], c_eye[0:8, 0:8],
                   bpT, [bSM])
                act(ltok[:, tb * 8:(tb + 1) * 8], pT[0:128, 0:8], AF.Copy, bSM, [bpT])
            lmax = SM[:, 32:36]
            nmax = SM[:, 36:40]
            ee = SM[:, 40:72]
            ssum = SM[:, 72:76]
            maskf = SM[:, 80:112]
            cumf = SM[:, 112:144]
            eidt = SM[:, 144:176]
            for tb in range(4):
                lt = ltok[:, tb * 8:(tb + 1) * 8]
                vred(lmax[:, tb:tb + 1], lt, True, bSM, [bSM])
                p.op("v", lambda v, o=nmax[:, tb:tb + 1], i=lt:
                     v.reduce_max(o, i, axis=AX.X, negate=True),
                     reads=[bSM], writes=[bSM])
            for tb in range(4):
                lt = ltok[:, tb * 8:(tb + 1) * 8]
                act(ee[:, tb * 8:(tb + 1) * 8], lt, AF.Exp, bSM, [bSM],
                    bias=nmax[:, tb:tb + 1])
                vts(maskf[:, tb * 8:(tb + 1) * 8], lt, lmax[:, tb:tb + 1], None,
                    ALU.is_equal, ALU.bypass, bSM, [bSM])
            for tb in range(4):
                vred(ssum[:, tb:tb + 1], ee[:, tb * 8:(tb + 1) * 8], False, bSM, [bSM])
            for tb in range(4):
                vscan(cumf[:, tb * 8:(tb + 1) * 8], c_ones[:, 0:8],
                      maskf[:, tb * 8:(tb + 1) * 8], c_z[:, 0:1], bSM, [bSM, bconst])
            for tb in range(4):
                vts(eidt[:, tb * 8:(tb + 1) * 8], cumf[:, tb * 8:(tb + 1) * 8], 1.0,
                    None, ALU.is_equal, ALU.bypass, bSM, [bSM])
            for tb in range(4):
                mf = maskf[:, tb * 8:(tb + 1) * 8]
                vtt(mf, mf, eidt[:, tb * 8:(tb + 1) * 8], ALU.mult, bSM, [bSM])
            maskT = SM[0:8, 1024:1536]
            for tb in range(4):
                tr(pT[0:8, 0:128], maskf[:, tb * 8:(tb + 1) * 8], c_eye[:], bpT, [bSM])
                act(maskT[:, tb * 128:(tb + 1) * 128], pT[0:8, 0:128], AF.Copy, bSM, [bpT])
            cumT = SM[0:8, 2048:2560]
            vscan(cumT, c_one8[0:8, 0:512], maskT, g_s8[:, 0:1], bSM, [bSM, bconst, bG])
            for tb in range(4):
                tr(pT[0:128, 0:8], cumT[:, tb * 128:(tb + 1) * 128], c_eye[0:8, 0:8],
                   bpT, [bSM])
                act(cumf[:, tb * 8:(tb + 1) * 8], pT[0:128, 0:8], AF.Copy, bSM, [bpT])
            vcopy(g_s8[:, 0:1], cumT[:, 511:512], bG, [bSM])
            rkt = SM[:, 176:208]
            for tb in range(4):
                mf = maskf[:, tb * 8:(tb + 1) * 8]
                vtt(eidt[:, tb * 8:(tb + 1) * 8], mf, c_evrow8[:], ALU.mult, bSM,
                    [bSM, bconst])
                vtt(rkt[:, tb * 8:(tb + 1) * 8], mf, cumf[:, tb * 8:(tb + 1) * 8],
                    ALU.mult, bSM, [bSM])
            for tb in range(4):
                j = sup * 4 + tb
                vred(g_eid[:, j:j + 1], eidt[:, tb * 8:(tb + 1) * 8], False, bG, [bSM])
                vred(g_rank[:, j:j + 1], rkt[:, tb * 8:(tb + 1) * 8], False, bG, [bSM])
                vcopy(g_ssb[:, j:j + 1], ssum[:, tb:tb + 1], bG, [bSM])

        def vspace(n=4):
            for i in range(n):
                vcopy(c_sp[:, (i % 4) * 8:(i % 4) * 8 + 8], c_ones[:, 0:8],
                      B_("sp"), [bconst])

        def unit_plan():
            s8 = g_s8
            vts(s8[:, 1:2], s8[:, 0:1], 511.0, 1.0 / 512.0, ALU.add, ALU.mult, bG, [bG])
            vspace()
            memset("v", s8[:, 96:112], 0.0, bG)
            for k in range(1, 17):
                vts(s8[:, 95 + k:96 + k], s8[:, 1:2], float(k), None, ALU.is_ge,
                    ALU.bypass, bG, [bG])
            vspace()
            vred(s8[:, 2:3], s8[:, 96:112], False, bG, [bG])
            tr(pT[0:1, 0:8], s8[:, 2:3], c_eye[0:8, 0:8], bpT, [bG])
            act(s8[0:1, 8:16], pT[0:1, 0:8], AF.Copy, bG, [bpT])
            vscan(s8[0:1, 16:24], c_one8[0:1, 0:8], s8[0:1, 8:16], c_z[0:1, 0:1],
                  bG, [bG, bconst])
            vspace()
            vtt(s8[0:1, 24:32], s8[0:1, 16:24], s8[0:1, 8:16], ALU.subtract, bG, [bG])
            tr(pT[0:8, 0:1], s8[0:1, 24:32], c_eye[0:1, 0:1], bpT, [bG])
            act(s8[:, 4:5], pT[0:8, 0:1], AF.Copy, bG, [bpT])
            vtt(s8[:, 5:6], s8[:, 4:5], s8[:, 2:3], ALU.add, bG, [bG])
            vts(s8[:, 32:56], c_u24[:], s8[:, 4:5], None, ALU.is_ge, ALU.bypass,
                bG, [bconst, bG])
            vspace()
            vts(s8[:, 56:80], c_u24[:], s8[:, 5:6], None, ALU.is_lt, ALU.bypass,
                bG, [bconst, bG])
            vspace()
            vtt(s8[:, 32:56], s8[:, 32:56], s8[:, 56:80], ALU.mult, bG, [bG])
            vspace()
            vts(s8[:, 56:80], s8[:, 32:56], c_iotap[0:8, 0:1], None, ALU.mult,
                ALU.bypass, bG, [bG, bconst])
            vspace()
            mm(pT[0:1, 32:56], c_one8[:, 0:1], s8[:, 56:80], True, True, bpT,
               [bG, bconst])
            act(s8[0:1, 104:128], pT[0:1, 32:56], AF.Copy, bG, [bpT])
            mm(pT[:, 0:24], c_ones[0:1, :], s8[0:1, 104:128], True, True, bpT,
               [bG, bconst])
            act(g_expu[:], pT[:, 0:24], AF.Copy, bG, [bpT])
            mm(pT[:, 32:40], c_ones[0:1, :], s8[0:1, 24:32], True, True, bpT,
               [bG, bconst])
            act(g_vb[:], pT[:, 32:40], AF.Copy, bG, [bpT])
            memset("v", g_dest[:], 0.0, bG)
            for e in range(8):
                vts(SM[:, e * 64:(e + 1) * 64], g_eid[:], float(e), None,
                    ALU.is_equal, ALU.bypass, bSM, [bG])
            for e in range(8):
                vstt(g_dest[:], SM[:, e * 64:(e + 1) * 64], g_vb[:, e:e + 1],
                     g_dest[:], ALU.mult, ALU.add, bG, [bSM, bG])
            vspace()
            vts(g_dest[:], g_dest[:], 512.0, -1.0, ALU.mult, ALU.add, bG, [bG])
            vspace()
            vtt(g_dest[:], g_dest[:], g_rank[:], ALU.add, bG, [bG])
            vspace()
            vts(g_dest[:], g_dest[:], 0.0, float(UCAP * ST - 1), ALU.max, ALU.min,
                bG, [bG])
            vspace()
            vcopy(g_desti[:], g_dest[:], bG, [bG])
            act(SM[:, 512:576], g_ssb[:], AF.Ln, bSM, [bG])
            act(c_sp[:, 0:8], c_ones[:, 0:8], AF.Copy, B_("sp"), [bconst])
            act(c_sp[:, 8:16], c_ones[:, 0:8], AF.Copy, B_("sp"), [bconst])
            act(g_scale[:], SM[:, 512:576], AF.Exp, bG, [bSM], scale=c_k[:, 2:3],
                bias=c_k[:, 3:4])
            specs = ((128.0, 0, g_w1i, 1023.0),)
            for i, (mulc, iot, dst, hi) in enumerate(specs):
                vts(SM[:, i * 32:i * 32 + 24], g_expu[:], mulc, None, ALU.mult,
                    ALU.bypass, bSM, [bG])
            vspace()
            for i, (mulc, iot, dst, hi) in enumerate(specs):
                vts(SM[:, i * 32:i * 32 + 24], SM[:, i * 32:i * 32 + 24],
                    c_iotap[:, iot:iot + 1], None, ALU.add, ALU.bypass, bSM,
                    [bSM, bconst])
            vspace()
            for i, (mulc, iot, dst, hi) in enumerate(specs):
                vts(SM[:, i * 32:i * 32 + 24], SM[:, i * 32:i * 32 + 24], 0.0, hi,
                    ALU.max, ALU.min, bSM, [bSM])
            vspace()
            for i, (mulc, iot, dst, hi) in enumerate(specs):
                vcopy(dst[:], SM[:, i * 32:i * 32 + 24], bG, [bSM])

        xT = AF8[:, 0:4096]
        o_f32 = AF8[:, 4096:8192]
        ght = AF8bf[:, 16384:24576]
        xrs = [AF8[:, 12288:13312], AF8[:, 13312:14336]]
        bxr = [B_("xr0"), B_("xr1")]
        xos = [AF8[:, 0:1024], AF8[:, 1024:2048]]
        bxo = [B_("xo0"), B_("xo1")]
        w1buf = BF8[:, 0:16384]
        w2buf = BF8[:, 16384:32768]
        bw1s = B_("w1s")
        bw2s = B_("w2s")
        IOOA = bass.IndirectOffsetOnAxis

        def gather_w(u, which):
            buf, bb, src = ((w1buf, bw1s, d_W1B) if which == 1 else
                            (w2buf, bw2s, d_W2B))
            p.dma_fn("g", lambda g, o=buf, ix=g_w1i[:, u:u + 1], s=src:
                     g.indirect_dma_start(out=o, out_offset=None, in_=s,
                                          in_offset=IOOA(ix, 0), element_offset=0,
                                          bounds_check=None, oob_is_err=False),
                     reads=[bG], writes=[bb])

        def gather_b(u):
            for e in range(8):
                vts(c_ind[:, e:e + 1], g_expu[:, u:u + 1], float(e), None,
                    ALU.is_equal, ALU.bypass, B_("ind"), [bG])
            for e in range(8):
                if e == 0:
                    vts(c_b1u[:], c_b1all[:, 0:16], c_ind[:, 0:1], None,
                        ALU.mult, ALU.bypass, B_("b1u"), [bconst, B_("ind")])
                    vts(c_b2u[:], c_b2all[:, 0:8], c_ind[:, 0:1], None,
                        ALU.mult, ALU.bypass, B_("b2u"), [bconst, B_("ind")])
                else:
                    vstt(c_b1u[:], c_b1all[:, e * 16:(e + 1) * 16], c_ind[:, e:e + 1],
                         c_b1u[:], ALU.mult, ALU.add, B_("b1u"),
                         [bconst, B_("ind"), B_("b1u")])
                    vstt(c_b2u[:], c_b2all[:, e * 8:(e + 1) * 8], c_ind[:, e:e + 1],
                         c_b2u[:], ALU.mult, ALU.add, B_("b2u"),
                         [bconst, B_("ind"), B_("b2u")])

        def p1_src(sup):
            for tk in range(4):
                xtile = SM[:, 0:1024]
                p.dma("q", xtile, d_x[sup * ST + tk * 128: sup * ST + (tk + 1) * 128, :],
                      writes=[bSM])
                for cb in range(8):
                    tr(pT[:, 0:128], xtile[:, cb * 128:(cb + 1) * 128], c_eye[:],
                       bpT, [bSM])
                    act(xT[:, cb * 512 + tk * 128: cb * 512 + (tk + 1) * 128],
                        pT[:, 0:128], AF.Copy, bxT, [bpT])
            def get(ct):
                return xT[:, ct * 512:(ct + 1) * 512], bxT
            return get

        def p4A_src(sup):
            for ct in range(8):
                p.dma("q", xT[:, ct * 512:(ct + 1) * 512],
                      d_R2[ct * 128:(ct + 1) * 128, sup * ST:(sup + 1) * ST],
                      writes=[bxT])
            for tk in range(4):
                xr = xrs[tk % 2]
                for cb in range(8):
                    tr(pT[:, 0:128], xT[:, cb * 512 + tk * 128: cb * 512 + (tk + 1) * 128],
                       c_eye[:], bpT, [bxT])
                    act(xr[:, cb * 128:(cb + 1) * 128], pT[:, 0:128], AF.Copy,
                        bxr[tk % 2], [bpT])
                p.dma("g", d_X3[(sup * 4 + tk) * 128:(sup * 4 + tk + 1) * 128, :], xr,
                      reads=[bxr[tk % 2]], writes=[B_("dX3")])
            def get(ct):
                return xT[:, ct * 512:(ct + 1) * 512], bxT
            return get

        def pass_C(src_tm):
            for j in range(NTT):
                xr = xrs[j % 2]
                p.dma("q", xr, src_tm[j * 128:(j + 1) * 128, :], writes=[bxr[j % 2]],
                      reads=[B_("dX3")] if src_tm is d_X3 else [])
                if T_SCAT:
                    p.dma_fn("g", lambda g, i=xr, ix=g_desti[:, j:j + 1]:
                             g.indirect_dma_start(out=d_XSRT, out_offset=IOOA(ix, 0),
                                                  in_=i, in_offset=None,
                                                  element_offset=0, bounds_check=None,
                                                  oob_is_err=False),
                             reads=[bxr[j % 2], bG], writes=[B_("dXS")])
                else:
                    p.dma("g", d_XSRT[j * 128:(j + 1) * 128, :], xr,
                          reads=[bxr[j % 2], bG], writes=[B_("dXS")])

        PS = [(pA, bpA), (pB, bpB), (pC, bpC), (pX[0], bpX[0]), (pX[1], bpX[1]),
              (pX[2], bpX[2]), (pX[3], bpX[3]), (pT, bpT)]

        def unit_D(u):
            if u == 0:
                gather_w(0, 1)
                gather_w(0, 2)
            gather_b(u)
            for r in range(4):
                xr = xrs[r % 2]
                p.dma("q", xr, d_XSRT[(u * 4 + r) * 128:(u * 4 + r + 1) * 128, :],
                      writes=[bxr[r % 2]], reads=[B_("dXS")])
                for cb in range(8):
                    tr(pT[:, 0:128], xr[:, cb * 128:(cb + 1) * 128], c_eye[:],
                       bpT, [bxr[r % 2]])
                    act(xT[:, cb * 512 + r * 128: cb * 512 + (r + 1) * 128],
                        pT[:, 0:128], AF.Copy, bxT, [bpT])
            def get(ct):
                return xT[:, ct * 512:(ct + 1) * 512], bxT
            ln_norm(get, False, BF2a, bk1=(pC, bpC), bk2=(pT, bpT))
            for ht in range(16):
                hp, bhp = (pA, bpA) if ht % 2 == 0 else (pB, bpB)
                for ct in range(8):
                    mm(hp[:, 0:512],
                       w1buf[:, ct * 2048 + ht * 128: ct * 2048 + (ht + 1) * 128],
                       BF2a[:, ct * 512:(ct + 1) * 512],
                       ct == 0, ct == 7, bhp, [bw1s, blnbf])
                act(ght[:, ht * 512:(ht + 1) * 512], hp[:, 0:512], AF.Gelu,
                    bgh, [bhp, B_("b1u")], bias=c_b1u[:, ht:ht + 1])
            if u + 1 < UCAP:
                gather_w(u + 1, 1)
            for h2 in range(2):
                for ht in range(16):
                    for c2 in range(4):
                        ct2 = h2 * 4 + c2
                        mm(pX[c2][:, 0:512],
                           w2buf[:, ht * 1024 + ct2 * 128: ht * 1024 + (ct2 + 1) * 128],
                           ght[:, ht * 512:(ht + 1) * 512],
                           ht == 0, ht == 15, bpX[c2], [bw2s, bgh])
                for c2 in range(4):
                    ct2 = h2 * 4 + c2
                    vts(o_f32[:, ct2 * 512:(ct2 + 1) * 512], pX[c2][:, 0:512],
                        c_b2u[:, ct2:ct2 + 1], None, ALU.add, ALU.bypass,
                        boacc, [bpX[c2], B_("b2u")])
            if u + 1 < UCAP:
                gather_w(u + 1, 2)
            for r in range(4):
                xr = xrs[r % 2]
                for cb in range(8):
                    tr(pT[:, 0:128], o_f32[:, cb * 512 + r * 128: cb * 512 + (r + 1) * 128],
                       c_eye[:], bpT, [boacc])
                    act(xr[:, cb * 128:(cb + 1) * 128], pT[:, 0:128], AF.Copy,
                        bxr[r % 2], [bpT])
                p.dma("g", d_OSRT[(u * 4 + r) * 128:(u * 4 + r + 1) * 128, :], xr,
                      reads=[bxr[r % 2]], writes=[B_("dOS")])

        def pass_E(phase):
            for sup in range(NSUP):
                for tk in range(4):
                    j = sup * 4 + tk
                    xr = xrs[j % 2]
                    xo = xos[j % 2]
                    if T_UNS:
                        p.dma_fn("g", lambda g, o=xr, ix=g_desti[:, j:j + 1]:
                                 g.indirect_dma_start(out=o, out_offset=None,
                                                      in_=d_OSRT,
                                                      in_offset=IOOA(ix, 0),
                                                      element_offset=0,
                                                      bounds_check=None,
                                                      oob_is_err=False),
                                 reads=[bG, B_("dOS")], writes=[bxr[j % 2]])
                    else:
                        p.dma("g", xr, d_OSRT[j * 128:(j + 1) * 128, :],
                              reads=[bG, B_("dOS")], writes=[bxr[j % 2]])
                    p.dma("q", xo,
                          (d_x if phase == 1 else d_X3)[j * 128:(j + 1) * 128, :],
                          writes=[bxo[j % 2]],
                          reads=[B_("dX3")] if phase == 4 else [])
                    vstt(xr, xr, g_scale[:, j:j + 1], xo, ALU.mult, ALU.add,
                         bxr[j % 2], [bxr[j % 2], bxo[j % 2], bG])
                    if phase == 1:
                        for cb in range(8):
                            tr(pT[:, 0:128], xr[:, cb * 128:(cb + 1) * 128], c_eye[:],
                               bpT, [bxr[j % 2]])
                            act(o_f32[:, cb * 512 + tk * 128: cb * 512 + (tk + 1) * 128],
                                pT[:, 0:128], AF.Copy, boacc, [bpT])
                    else:
                        p.dma("g", d_out[j * 128:(j + 1) * 128, :], xr,
                              reads=[bxr[j % 2]])
                if phase == 1:
                    for cb in range(8):
                        p.dma("g", d_R1[cb * 128:(cb + 1) * 128, sup * ST:(sup + 1) * ST],
                              o_f32[:, cb * 512:(cb + 1) * 512], reads=[boacc],
                              writes=[B_("dR1")])

        def ffn_sparse(phase):
            src_fn = p1_src if phase == 1 else p4A_src
            memset("v", g_s8[:, 0:1], 0.0, bG)
            for sup in range(NSUP):
                get_x = src_fn(sup)
                lnstats_route(get_x)
                routing_A(sup)
            p.barrier()
            unit_plan()
            if DEBUG and phase == 1:
                for i, t in enumerate((g_eid, g_rank, g_ssb, g_dest, g_scale)):
                    p.dma("g", d_DBG[i][:, 0:64], t[:], reads=[bG])
                p.dma("g", d_DBG[5][:, 0:24], g_expu[:], reads=[bG])
                p.dma("g", d_DBG[6][0:8, 0:128], g_s8[:], reads=[bG])
                p.dma("g", d_DBG[7][:, 0:24], g_w1i[:].bitcast(F32), reads=[bG])
            p.barrier()
            pass_C(d_x if phase == 1 else d_X3)
            p.barrier()
            for u in range(UCAP):
                unit_D(u)
            p.barrier()
            pass_E(phase)
            p.barrier()

        ffn_sparse(1)

        bscr = B_("scr")
        br1 = B_("r1")
        for ct in range(8):
            scr = AF8[:, 0:4096]
            p.dma("q", scr, d_winT[:, ct * 4096:(ct + 1) * 4096], writes=[bscr])
            vcopy(BF8[:, ct * 4096:(ct + 1) * 4096], scr, bw1, [bscr])
        for sup in range(NSUP):
            b = (sup * ST) // L
            t0 = (sup * ST) % L
            r1 = AF8[:, 4096:8192]
            for ct in range(8):
                p.dma("q", r1[:, ct * 512:(ct + 1) * 512],
                      d_R1[ct * 128:(ct + 1) * 128, sup * ST:(sup + 1) * ST],
                      writes=[br1])
            def get2(ct, r1=r1):
                return r1[:, ct * 512:(ct + 1) * 512], br1
            ln_norm(get2, False, BF2a)
            for mt in range(32):
                tgt = d_XS if mt < 16 else d_ZS
                dt_ = mt % 16
                for ct in range(8):
                    mm(pA[:, 0:512],
                       BF8[:, ct * 4096 + mt * 128: ct * 4096 + (mt + 1) * 128],
                       BF2a[:, ct * 512:(ct + 1) * 512], ct == 0, ct == 7,
                       bpA, [bw1, blnbf])
                ev = SM[:, 0:512] if mt % 2 == 0 else SM[:, 512:1024]
                act(ev, pA[:, 0:512], AF.Copy, bSM, [bpA])
                p.dma("g", tgt[b, dt_ * 128:(dt_ + 1) * 128, t0:t0 + 512], ev,
                      reads=[bSM])
        p.barrier()

        for half in range(2):
            scr = AF8[:, 0:8192]
            p.dma("q", scr, d_woutT[:, half * 8192:(half + 1) * 8192], writes=[bscr])
            vcopy(BF8[:, half * 8192:(half + 1) * 8192], scr, bw1, [bscr])
        woutT = BF8[:, 0:16384]
        xcbf = BF8[:, 16384:24576]
        yszb = BF8[:, 24576:32768]
        bxcbf = B_("xcbf")
        bysz = B_("ysz")

        XU = AF8[:, 0:8256]
        bXU = B_("XU")
        SLOT = [AF8[:, 8256 + i * 512: 8256 + (i + 1) * 512] for i in range(12)]
        bSL = [B_(f"sl{i}") for i in range(12)]
        Cb = BF2a
        bCb = B_("CbP3")
        BF8f = BF8.bitcast(F32)
        sbs = [BF2b.bitcast(F32)[:, 0:512], BF2b.bitcast(F32)[:, 512:1024],
               BF8f[:, 16384:16896], BF8f[:, 16896:17408]]
        bsb = [B_("sb0"), B_("sb1"), B_("sb2"), B_("sb3")]
        h_ng = SM[:, 0:2048]
        bh = B_("h_ng")
        zero512 = SM[:, 2048:2560]
        bzero = B_("zero")
        dtc_bf = BFS[0:64, 0:512]
        bdtc = B_("dtc")
        Cv_bf = BFS[0:16, 512:1024]
        bCv = B_("Cv")
        Bv_bf = BFS[0:16, 1024:1536]
        bBv = B_("Bv")
        hC_big = BFS[:, 1536:3584]
        bhCb = B_("hCb")
        uDb = BFS[:, 3584:4096]
        buD = B_("uD")
        fin8 = U8a[:, 0:512]
        m28 = U8a[:, 512:1024]
        bu8f = B_("u8fin")
        bu8m = B_("u8m2")

        memset("v", zero512, 0.0, bzero)

        for b in range(B_LOC):
            for rev in (0, 1):
                dirp = rev
                memset("v", c_carry[:], 0.0, bcarry)
                chunks = range(NCH) if not rev else range(NCH - 1, -1, -1)
                for ci in chunks:
                    c0 = ci * CH
                    for dt_ in range(16):
                        xv = XU[:, dt_ * 516: dt_ * 516 + 516]
                        if not rev:
                            if c0 == 0:
                                memset("v", xv[:, 0:3], 0.0, bXU)
                                p.dma("q", xv[:, 3:515],
                                      d_XS[b, dt_ * 128:(dt_ + 1) * 128, 0:512],
                                      writes=[bXU])
                            else:
                                p.dma("q", xv[:, 0:515],
                                      d_XS[b, dt_ * 128:(dt_ + 1) * 128, c0 - 3:c0 + 512],
                                      writes=[bXU])
                        else:
                            if c0 + CH + 3 > L:
                                memset("v", xv[:, 512:515], 0.0, bXU)
                                p.dma("q", xv[:, 0:512],
                                      d_XS[b, dt_ * 128:(dt_ + 1) * 128, c0:c0 + 512],
                                      writes=[bXU])
                            else:
                                p.dma("q", xv[:, 0:515],
                                      d_XS[b, dt_ * 128:(dt_ + 1) * 128, c0:c0 + 515],
                                      writes=[bXU])
                    for dt_ in range(16):
                        xv = XU[:, dt_ * 516: dt_ * 516 + 516]
                        xc = SLOT[0]
                        for j in range(4):
                            wj = (j if not rev else 3 - j)
                            wcol = c_convw[:, dirp * 64 + dt_ * 4 + wj:
                                           dirp * 64 + dt_ * 4 + wj + 1]
                            if j == 0:
                                vts(xc, xv[:, 0:512], wcol, None, ALU.mult, ALU.bypass,
                                    bSL[0], [bXU, bconst])
                            else:
                                vstt(xc, xv[:, j:j + 512], wcol, xc, ALU.mult, ALU.add,
                                     bSL[0], [bXU, bconst, bSL[0]])
                        cbb = c_convb[:, dirp * 16 + dt_: dirp * 16 + dt_ + 1]
                        vts(xcbf[:, dt_ * 512:(dt_ + 1) * 512], xc, cbb, None,
                            ALU.add, ALU.bypass, bxcbf, [bSL[0], bconst])
                        vts(SLOT[1], xc, cbb, 10.0, ALU.add, ALU.min, bSL[1],
                            [bSL[0], bconst])
                        vts(XU[:, dt_ * 516: dt_ * 516 + 512], SLOT[1], -10.0, None,
                            ALU.max, ALU.bypass, bXU, [bSL[1]])
                    xo = dirp * 1536
                    for (m0, m1, ps_, bps) in ((0, 64, pA, bpA), (64, 80, pB, bpB),
                                               (80, 96, pC, bpC)):
                        for dt_ in range(16):
                            mm(ps_[0:m1 - m0, 0:512],
                               c_xprojT[:, xo + dt_ * 96 + m0: xo + dt_ * 96 + m1],
                               xcbf[:, dt_ * 512:(dt_ + 1) * 512],
                               dt_ == 0, dt_ == 15, bps, [bconst, bxcbf])
                    act(dtc_bf, pA[0:64, 0:512], AF.Copy, bdtc, [bpA])
                    act(Bv_bf, pB[0:16, 0:512], AF.Copy, bBv, [bpB])
                    act(Cv_bf, pC[0:16, 0:512], AF.Copy, bCv, [bpC])
                    for n in range(16):
                        mm(pT[:, 0:512], c_sel[:, n * 128:(n + 1) * 128], Cv_bf,
                           True, True, bpT, [bconst, bCv])
                        act(Cb[:, n * 512:(n + 1) * 512], pT[:, 0:512], AF.Copy,
                            bCb, [bpT])
                    for dt_ in range(16):
                        u_dt = XU[:, dt_ * 516: dt_ * 516 + 512]
                        mm(pA[:, 0:512],
                           c_dtprojT[:, dirp * 2048 + dt_ * 128: dirp * 2048 + (dt_ + 1) * 128],
                           dtc_bf, True, True, bpA, [bconst, bdtc])
                        dtb_ = c_dtb[:, dirp * 16 + dt_: dirp * 16 + dt_ + 1]
                        vts(SLOT[1], pA[:, 0:512], dtb_, None, ALU.add, ALU.bypass,
                            bSL[1], [bpA, bconst])
                        vts(SLOT[1], SLOT[1], 1e-5, 1.0, ALU.max, ALU.min, bSL[1], [bSL[1]])
                        act(SLOT[1], SLOT[1], AF.Exp, bSL[1], [bSL[1]], bias=dtb_)
                        delta = SLOT[2]
                        act(delta, SLOT[1], AF.Ln, bSL[2], [bSL[1]], bias=1.0)
                        du = SLOT[3]
                        vtt(du, delta, u_dt, ALU.mult, bSL[3], [bSL[2], bXU])
                        dAs = [SLOT[4], SLOT[5], SLOT[6], SLOT[7]]
                        bdAs = [bSL[4], bSL[5], bSL[6], bSL[7]]
                        PXL = [(pX[0], bpX[0]), (pX[1], bpX[1]), (pX[2], bpX[2]),
                               (pX[3], bpX[3]), (pA, bpA), (pC, bpC), (pT, bpT)]

                        def px_bcast(ng):
                            for nl in range(4):
                                n = ng * 4 + nl
                                pxk, bpxk = PXL[n % 7]
                                mm(pxk[:, 0:512], c_sel[:, n * 128:(n + 1) * 128],
                                   Bv_bf, True, True, bpxk, [bconst, bBv])

                        for ng in range(4):
                            if ng == 0:
                                px_bcast(0)
                            for nl in range(4):
                                n = ng * 4 + nl
                                act(dAs[nl], delta, AF.Exp, bdAs[nl], [bSL[2]],
                                    scale=c_A[:, dirp * 256 + dt_ * 16 + n:
                                              dirp * 256 + dt_ * 16 + n + 1])
                            for nl in range(4):
                                n = ng * 4 + nl
                                pxk, bpxk = PXL[n % 7]
                                vtt(sbs[nl], du, pxk[:, 0:512], ALU.mult, bsb[nl],
                                    [bSL[3], bpxk])
                            if ng < 3:
                                px_bcast(ng + 1)
                            for nl in range(4):
                                n = ng * 4 + nl
                                hsl = h_ng[:, nl * 512:(nl + 1) * 512]
                                init = c_carry[:, dt_ * 16 + n: dt_ * 16 + n + 1]
                                if not rev:
                                    vscan(hsl, dAs[nl], sbs[nl], init, bh,
                                          [bdAs[nl], bsb[nl], bcarry])
                                else:
                                    vscan(hsl[:, ::-1], dAs[nl][:, ::-1],
                                          sbs[nl][:, ::-1], init, bh,
                                          [bdAs[nl], bsb[nl], bcarry])
                            vtt(hC_big[:, 0:2048], h_ng[:, 0:2048],
                                Cb[:, ng * 2048:(ng + 1) * 2048], ALU.mult, bhCb,
                                [bh, bCb])
                            for nl in range(4):
                                mm(pB[:, 0:512], c_eyebf[:],
                                   hC_big[:, nl * 512:(nl + 1) * 512],
                                   (ng == 0 and nl == 0), False, bpB, [bconst, bhCb])
                            lastc = 511 if not rev else 0
                            vcopy(c_carry[:, dt_ * 16 + ng * 4: dt_ * 16 + ng * 4 + 4],
                                  h_ng[:, lastc::512], bcarry, [bh])
                        vts(uDb, u_dt, c_Dv[:, dirp * 16 + dt_: dirp * 16 + dt_ + 1],
                            None, ALU.mult, ALU.bypass, buD, [bXU, bconst])
                        mm(pB[:, 0:512], c_eyebf[:], uDb, False, True, bpB,
                           [bconst, buD])
                        y1 = SLOT[9]
                        act(y1, pB[:, 0:512], AF.Copy, bSL[9], [bpB])
                        if not rev:
                            p.dma("g", d_YF[b, dt_ * 128:(dt_ + 1) * 128, c0:c0 + 512],
                                  y1, reads=[bSL[9]])
                        else:
                            yf = SLOT[6]
                            zt = SLOT[7]
                            p.dma("q", yf, d_YF[b, dt_ * 128:(dt_ + 1) * 128, c0:c0 + 512],
                                  writes=[bSL[6]])
                            p.dma("q", zt, d_ZS[b, dt_ * 128:(dt_ + 1) * 128, c0:c0 + 512],
                                  writes=[bSL[7]])
                            vtt(y1, y1, yf, ALU.add, bSL[9], [bSL[9], bSL[6]])
                            sz = SLOT[8]
                            act(sz, zt, AF.Tanh, bSL[8], [bSL[7]], scale=c_k[:, 0:1])
                            vts(sz, sz, 0.5, 0.5, ALU.mult, ALU.add, bSL[8], [bSL[8]])
                            vtt(sz, sz, zt, ALU.mult, bSL[8], [bSL[8], bSL[7]])
                            vtt(yszb[:, dt_ * 512:(dt_ + 1) * 512], y1, sz, ALU.mult,
                                bysz, [bSL[9], bSL[8]])
                    if rev:
                        for ct in range(8):
                            op_, bop = (pA, bpA) if ct % 2 == 0 else (pC, bpC)
                            for dt_ in range(16):
                                mm(op_[:, 0:512],
                                   woutT[:, dt_ * 1024 + ct * 128: dt_ * 1024 + (ct + 1) * 128],
                                   yszb[:, dt_ * 512:(dt_ + 1) * 512],
                                   dt_ == 0, dt_ == 15, bop, [bw1, bysz])
                            o1 = SLOT[10]
                            act(o1, op_[:, 0:512], AF.Copy, bSL[10], [bop])
                            act(fin8, op_[:, 0:512], AF.Is_finite, bu8f, [bop])
                            sg = SLOT[8]
                            act(sg, op_[:, 0:512], AF.Sign, bSL[8], [bop])
                            s1 = SLOT[1]
                            p.op("v", lambda v, o=s1, m=fin8, a=o1, bb=sg:
                                 v.select(o, m, a, bb),
                                 reads=[bu8f, bSL[10], bSL[8]], writes=[bSL[1]])
                            vtt(m28, s1, s1, ALU.is_equal, bu8m, [bSL[1]])
                            s2 = SLOT[10]
                            p.op("v", lambda v, o=s2, m=m28, a=s1, bb=zero512:
                                 v.select(o, m, a, bb),
                                 reads=[bu8m, bSL[1], bzero], writes=[bSL[10]])
                            r1t = SLOT[11]
                            p.dma("q", r1t, d_R1[ct * 128:(ct + 1) * 128,
                                                 b * L + c0: b * L + c0 + 512],
                                  writes=[bSL[11]])
                            x2 = SLOT[1]
                            vtt(x2, s2, r1t, ALU.add, bSL[1], [bSL[10], bSL[11]])
                            p.dma("g", d_R2[ct * 128:(ct + 1) * 128,
                                            b * L + c0: b * L + c0 + 512], x2,
                                  reads=[bSL[1]])
                p.barrier()
        p.barrier()

        ffn_sparse(4)

        p.cmds["g"].append([lambda eng: eng.wait_ge(sems["g"], p.cnt["g"]), None])
        p.cmds["q"].append([lambda eng: eng.wait_ge(sems["q"], p.cnt["q"]), None])

        block = stk.enter_context(nc.Block())

        def replay(eng_obj, ename):
            for fn, inc in p.cmds[ename]:
                ins = fn(eng_obj)
                if inc:
                    ins.then_inc(sems[ename], inc)

        @block.tensor
        def _(t):
            replay(t, "t")

        @block.vector
        def _(v):
            replay(v, "v")

        @block.scalar
        def _(s):
            replay(s, "s")

        @block.sync
        def _(q):
            replay(q, "q")

        @block.gpsimd
        def _(g):
            replay(g, "g")

    return nc


def _prep_inputs(inp):
    g = {k: np.asarray(v) for k, v in inp.items()}
    f = np.float32
    out = {}
    out["winT"] = np.ascontiguousarray(
        g["L0_attn_in_proj_w"].T.reshape(8, 128, 4096).transpose(1, 0, 2)
        .reshape(128, 8 * 4096)).astype(f, copy=False)
    out["woutT"] = np.ascontiguousarray(
        g["L0_attn_out_proj_w"].T.reshape(16, 128, 1024).transpose(1, 0, 2)
        .reshape(128, 16 * 1024)).astype(f, copy=False)
    xp = np.stack([np.asarray(g["L0_attn_x_proj_w"]).T,
                   np.asarray(g["L0_attn_x_proj_b_w"]).T])
    out["xprojT"] = np.ascontiguousarray(
        xp.reshape(2, 16, 128, 96).transpose(2, 0, 1, 3).reshape(128, 2 * 1536)
    ).astype(f, copy=False)
    dtp = np.stack([np.asarray(g["L0_attn_dt_proj_w"]).T,
                    np.asarray(g["L0_attn_dt_proj_b_w"]).T])
    out["dtprojT"] = np.ascontiguousarray(
        dtp.transpose(1, 0, 2).reshape(64, 2 * 2048)).astype(f, copy=False)
    cw = np.stack([g["L0_attn_conv1d_w"].reshape(2048, 4),
                   g["L0_attn_conv1d_b_w"].reshape(2048, 4)])
    out["convw"] = np.ascontiguousarray(
        cw.reshape(2, 16, 128, 4).transpose(2, 0, 1, 3).reshape(128, 2 * 64)
    ).astype(f, copy=False)
    def col2(a, b):
        s = np.stack([np.asarray(a).reshape(16, 128), np.asarray(b).reshape(16, 128)])
        return np.ascontiguousarray(s.transpose(2, 0, 1).reshape(128, 32)).astype(f, copy=False)
    out["convb"] = col2(g["L0_attn_conv1d_bias"], g["L0_attn_conv1d_b_bias"])
    out["dtb"] = col2(g["L0_attn_dt_proj_bias"], g["L0_attn_dt_proj_b_bias"])
    out["Dv"] = col2(g["L0_attn_D"], g["L0_attn_D_b"])
    al = np.stack([g["L0_attn_A_log"], g["L0_attn_A_b_log"]])
    out["Alog"] = np.ascontiguousarray(
        al.reshape(2, 16, 128, 16).transpose(2, 0, 1, 3).reshape(128, 512)
    ).astype(f, copy=False)
    out["wred"] = np.ascontiguousarray(
        g["L0_ffn_w_red"].reshape(8, 128, 16).transpose(1, 0, 2).reshape(128, 128)
    ).astype(f, copy=False)
    out["wg"] = np.ascontiguousarray(g["L0_ffn_wg"]).astype(f, copy=False)
    out["W1"] = np.ascontiguousarray(g["L0_ffn_W1"]).astype(f, copy=False)
    out["W2"] = np.ascontiguousarray(g["L0_ffn_W2"]).astype(f, copy=False)
    out["b1c"] = np.ascontiguousarray(
        g["L0_ffn_b1"].reshape(8, 16, 128).transpose(2, 0, 1).reshape(128, 128)
    ).astype(f, copy=False)
    out["b2c"] = np.ascontiguousarray(
        g["L0_ffn_b2"].reshape(8, 8, 128).transpose(2, 0, 1).reshape(128, 64)
    ).astype(f, copy=False)
    out["iotap"] = np.stack([np.arange(128), 8 * np.arange(128)], axis=1).astype(f)
    out["evrow8"] = np.tile(np.arange(8).astype(f), (128, 1))
    out["u24"] = np.tile(np.arange(24).astype(f), (8, 1))
    out["eye_f32"] = np.eye(128, dtype=f)
    out["eye_bf"] = np.eye(128).astype(BFNP)
    out["ones_f32"] = np.ones((128, 128), f)
    out["ones_bf"] = np.ones((16, 512), BFNP)
    sel = np.zeros((16, 16 * 128), np.float32)
    for n in range(16):
        sel[n, n * 128:(n + 1) * 128] = 1.0
    out["sel_bf"] = sel.astype(BFNP)
    return out


def kernel(**inp):
    shared = _prep_inputs(inp)
    x = np.ascontiguousarray(
        np.asarray(inp["L0_attn_hidden_states"], dtype=np.float32).reshape(65536, 1024))
    nc = build_nc()
    in_maps = []
    for c in range(NCORE):
        m = dict(shared)
        m["x"] = np.ascontiguousarray(x[c * TOK:(c + 1) * TOK])
        in_maps.append(m)
    res = run_bass_kernel_spmd(nc, in_maps, core_ids=list(range(NCORE)))
    out = np.concatenate([np.asarray(r["out"]) for r in res.results], axis=0)
    return out.reshape(16, 4096, 1024)


if __name__ == "__main__":
    build_nc()
    print("built ok")
